# version 06
# baseline (speedup 1.0000x reference)
import math
import numpy as np
import ml_dtypes
import concourse.bass as bass
import concourse.mybir as mybir
from concourse.bass import AP
from concourse.bass_utils import run_bass_kernel_spmd

F32 = mybir.dt.float32
BF16 = mybir.dt.bfloat16
FP16 = mybir.dt.float16
AF = mybir.ActivationFunctionType
ALU = mybir.AluOpType

S = 4096
Dm = 1024
H = 16
NT = 32
NCT = 8
NQC = 8
CEXP = 8.0


def _bucket(delta):
    n = max(delta, 0)
    if n == 0:
        return 0
    if n < 16:
        return n
    return min(16 + int(math.log(n / 16.0) / math.log(2.0) * 16.0), 31)


def _host_consts():
    oh = np.zeros((32, 512), dtype=np.float32)
    for j in range(512):
        oh[_bucket(j - 128), j] += 1.0
        oh[31, j] -= 1.0
    jmat = np.zeros((128, 128), dtype=np.float32)
    for i in range(128):
        jmat[i, 127 - i] = 1.0
    imat = np.eye(128, dtype=np.float32)
    return oh, jmat, imat


class Ctr:
    def __init__(self, sem):
        self.sem = sem
        self.n = 0

    def inc(self, inst, k=1):
        inst.then_inc(self.sem, k)
        self.n += k
        return self.n


def _kt_class(kt, qc):
    qlo, qhi = 4 * qc, 4 * qc + 3
    if kt > qhi:
        return "fut"
    if kt >= qlo - 1:
        return "diag"
    return "past"


def _preload_named_set(nc, eng, name):
    from concourse.hw_specs import get_activation_tables
    for idx, nm in enumerate(get_activation_tables(nc.m.arch)):
        if nm == name:
            return eng.add_instruction(mybir.InstLoadActFuncSet(
                name=nc.get_next_instruction_name(), act_func_set_id=idx,
                ins=[], outs=[]))
    raise ValueError(name)


def build_nc(lvl=7, dump=None):
    nc = bass.Bass("TRN2", target_bir_lowering=False)

    x_h = nc.dram_tensor("x", [S, Dm], F32, kind="ExternalInput")
    wq_h = nc.dram_tensor("wq", [Dm, Dm], F32, kind="ExternalInput")
    bq_h = nc.dram_tensor("bq", [Dm], F32, kind="ExternalInput")
    wk_h = nc.dram_tensor("wk", [Dm, Dm], F32, kind="ExternalInput")
    bk_h = nc.dram_tensor("bk", [Dm], F32, kind="ExternalInput")
    wv_h = nc.dram_tensor("wv", [Dm, Dm], F32, kind="ExternalInput")
    bv_h = nc.dram_tensor("bv", [Dm], F32, kind="ExternalInput")
    rb_h = nc.dram_tensor("relb", [32, H], F32, kind="ExternalInput")
    kr_h = nc.dram_tensor("ker", [Dm, Dm], F32, kind="ExternalInput")
    fb_h = nc.dram_tensor("fbias", [Dm], F32, kind="ExternalInput")
    a_h = nc.dram_tensor("a_w", [Dm, 16], F32, kind="ExternalInput")
    b_h = nc.dram_tensor("b_w", [16, Dm], F32, kind="ExternalInput")
    oh_h = nc.dram_tensor("onehot", [32, 512], F32, kind="ExternalInput")
    jm_h = nc.dram_tensor("jmat", [128, 128], F32, kind="ExternalInput")
    im_h = nc.dram_tensor("imat", [128, 128], F32, kind="ExternalInput")
    out_h = nc.dram_tensor("out", [S, Dm], F32, kind="ExternalOutput")
    dbg_h = nc.dram_tensor("dbg", [128, 64], F32, kind="ExternalOutput")
    dbg2_h = nc.dram_tensor("dbg2", [128, 4096], F32, kind="ExternalOutput")

    xn_hb = nc.dram_tensor("xn_hb", [S, Dm], BF16, kind="Internal")
    strip_hb = nc.dram_tensor("strip_hb", [H, 512], F32, kind="Internal")
    qT_hb = nc.dram_tensor("qT_hb", [NCT, 128, S], BF16, kind="Internal")
    kT_hb = nc.dram_tensor("kT_hb", [NCT, 128, S], BF16, kind="Internal")
    v_hb = nc.dram_tensor("v_hb", [4, NT, 128, 256], BF16, kind="Internal")
    g_hb = nc.dram_tensor("g_hb", [NT, 128, Dm], BF16, kind="Internal")
    sT_hb = nc.dram_tensor("sT_hb", [Dm, S], BF16, kind="Internal")
    den_hb = nc.dram_tensor("den_hb", [H, S], FP16, kind="Internal")
    sc_hb = nc.dram_tensor("sc_hb", [4], F32, kind="Internal")

    N_UT, N_FFN, N_QK, N_V = NQC, NT * 2, NCT * NQC, NT * 2
    N_CH = N_UT + N_FFN + 2 * N_QK + N_V

    def pst_after_store(idx_kind, k):
        if idx_kind == "g":
            return (k + 1) * 16
        if idx_kind == "q":
            return (N_FFN + k + 1) * 16
        if idx_kind == "k":
            return (N_FFN + N_QK + k + 1) * 16
        return (N_FFN + 2 * N_QK + 2 * (k + 1)) * 16

    PST_ALL = (N_FFN + 2 * N_QK + 2 * N_V) * 16

    import contextlib
    es = contextlib.ExitStack()
    with es:
        ring = es.enter_context(nc.psum_tensor("ring", [128, 2048], F32))
        denp = es.enter_context(nc.psum_tensor("denp", [128, 512], F32))
        ctxA = es.enter_context(nc.psum_tensor("ctxA", [128, 512], F32))
        ctxB = es.enter_context(nc.psum_tensor("ctxB", [128, 512], F32))
        prod = es.enter_context(nc.psum_tensor("prod", [128, 512], F32))
        ccol = es.enter_context(nc.sbuf_tensor("ccol", [128, 16], F32))
        hcol = es.enter_context(nc.sbuf_tensor("hcol", [128, H], F32))
        bqs = es.enter_context(nc.sbuf_tensor("bqs", [128, NCT], F32))
        bks = es.enter_context(nc.sbuf_tensor("bks", [128, NCT], F32))
        bvrow = es.enter_context(nc.sbuf_tensor("bvrow", [1, Dm], BF16))
        relb_s = es.enter_context(nc.sbuf_tensor("relb_s", [32, H], F32))
        oh_s = es.enter_context(nc.sbuf_tensor("oh_s", [32, 512], F32))
        strip_s = es.enter_context(nc.sbuf_tensor("strip_s", [16, 512], F32))
        jmat_s = es.enter_context(nc.sbuf_tensor("jmat_s", [128, 128], BF16))
        imat_s = es.enter_context(nc.sbuf_tensor("imat_s", [128, 128], F32))
        twos_s = es.enter_context(nc.sbuf_tensor("twos_s", [128, 32], BF16))
        onesr = es.enter_context(nc.sbuf_tensor("onesr", [1, 512], BF16))
        scol = es.enter_context(nc.sbuf_tensor("scol", [128, 4], F32))
        sc1 = es.enter_context(nc.sbuf_tensor("sc1", [1, 64], F32))
        sems = [es.enter_context(nc.semaphore("s_%d" % i)) for i in range(27)]
        (_s0, _s1, _s2, _s3, _s4, _s5, _s6, _s7, _s8, _s9, _s10, _s11, _s12,
         _s13, _s14, _s15, _s16, _s17, _s18, _s19, _s20, _s21, _s22,
         _s23, _s24, _s25, _s26) = sems
        block = es.enter_context(nc.Block())
        pre, lnx, v1, rst, v2 = Ctr(_s0), Ctr(_s1), Ctr(_s2), Ctr(_s3), Ctr(_s4)
        xnw, xnt, qnt, prd, pev = Ctr(_s5), Ctr(_s6), Ctr(_s7), Ctr(_s8), Ctr(_s9)
        pst, stld, qld, sc, ex = Ctr(_s10), Ctr(_s11), Ctr(_s12), Ctr(_s13), Ctr(_s14)
        cns, epi, est, eld, ecp = Ctr(_s15), Ctr(_s16), Ctr(_s17), Ctr(_s18), Ctr(_s19)
        ost, aux, btl = Ctr(_s20), Ctr(_s21), Ctr(_s22)
        vsf, ssf = Ctr(_s23), Ctr(_s24)
        dbgw = Ctr(_s25)
        dtx = Ctr(_s26)


        es1 = contextlib.ExitStack()
        with es1:
            xnT = es1.enter_context(nc.sbuf_tensor("xnT", [128, NCT * S], BF16))
            qint = es1.enter_context(nc.sbuf_tensor("qint", [128, NCT * 1024], BF16))
            wbuf = es1.enter_context(nc.sbuf_tensor("wbuf", [128, NCT * 1024], BF16))
            krf = es1.enter_context(nc.sbuf_tensor("krf", [128, NCT * 1024], F32))
            xr = es1.enter_context(nc.sbuf_tensor("xr", [128, 4 * 1024], F32))
            xnr = es1.enter_context(nc.sbuf_tensor("xnr", [128, 4 * 1024], BF16))
            junk = es1.enter_context(nc.sbuf_tensor("junk", [128, 1024], F32))
            lncols = es1.enter_context(nc.sbuf_tensor("lncols", [128, 4 * 8], F32))
            amx = es1.enter_context(nc.sbuf_tensor("amx", [128, 10], F32))
            evst = es1.enter_context(nc.sbuf_tensor("evst", [128, 3 * 512], BF16))
            baug = es1.enter_context(nc.sbuf_tensor("baug", [17, 1024], F32))
            baugb = es1.enter_context(nc.sbuf_tensor("baugb", [17, 1024], BF16))
            uTa = es1.enter_context(nc.sbuf_tensor("uTa", [17, S], BF16))
            a_s = es1.enter_context(nc.sbuf_tensor("a_s", [128, NCT * 16], BF16))
            th_s = es1.enter_context(nc.sbuf_tensor("th_s", [128, 512], F32))
            g_ring = es1.enter_context(nc.sbuf_tensor("g_ring", [128, 4 * 512], BF16))
            def p0_g(g):
                pre.inc(g.dma_start(out=relb_s[:], in_=rb_h.ap()), 16)
                pre.inc(g.dma_start(out=oh_s[:], in_=oh_h.ap()), 16)
                pre.inc(g.dma_start(out=jmat_s[:], in_=jm_h.ap()), 16)
                pre.inc(g.dma_start(out=imat_s[:], in_=im_h.ap()), 16)
                pre.inc(g.dma_start(out=bqs[:],
                                    in_=AP(bq_h, 0, [[1, 128], [128, NCT]]),
                                    allow_slow_non_contiguous=True), 16)
                pre.inc(g.dma_start(out=bks[:],
                                    in_=AP(bk_h, 0, [[1, 128], [128, NCT]]),
                                    allow_slow_non_contiguous=True), 16)
                pre.inc(g.dma_start(out=bvrow[:],
                                    in_=AP(bv_h, 0, [[0, 1], [1, Dm]])), 16)
                pre.inc(g.dma_start(
                    out=a_s[:], in_=AP(a_h, 0, [[16, 128], [2048, NCT], [1, 16]])), 16)
                pre.inc(g.dma_start(out=baug[0:16, :], in_=b_h.ap()), 16)
                pre.inc(g.dma_start(out=baug[16:17, :],
                                    in_=AP(fb_h, 0, [[0, 1], [1, Dm]])), 16)
                g.wait_ge(pre.sem, 161)
                pre.inc(g.dma_start(
                    out=uTa[16:17, :],
                    in_=AP(onesr, 0, [[512, 1], [0, 8], [1, 512]])), 16)
            block.gpsimd(p0_g)
            PRE_LOADS = 160

            def p0_v(v):
                v.memset(twos_s[:], 2.0)
                v.memset(onesr[:], 1.0)
                v.memset(ccol[:, 0:1], -CEXP)
                v.memset(ccol[:, 1:2], 1.0 / Dm)
                v.memset(ccol[:, 2:3], 1e-5)
                v.memset(ccol[:, 3:4], 1.0)
                v.memset(ccol[:, 4:5], -0.5)
                v.memset(ccol[:, 5:6], -1.0)
                v.memset(ccol[:, 6:7], 0.125)
                v.memset(ccol[:, 7:8], 7.0)
                v.memset(ccol[:, 8:9], 12582912.0)
                v.memset(ccol[:, 9:10], -12582912.0)
                v.memset(ccol[:, 10:11], -8.0)
                v.memset(ccol[:, 11:12], 0.5)
                v.memset(ccol[:, 12:13], 0.0)
                v.memset(ccol[:, 13:14], 1.0 / 14.0)
                pre.inc(v.memset(ccol[:, 14:15], 0.0))
            block.vector(p0_v)
            PRE_ALL = PRE_LOADS + 1 + 16

            def p1_sy(sy):
                for st in range(NT):
                    if st >= 4:
                        sy.wait_ge(v2.sem, (st - 4) + 1)
                    lnx.inc(sy.dma_start(
                        out=xr[:, (st % 4) * 1024:(st % 4 + 1) * 1024],
                        in_=x_h.ap()[st * 128:(st + 1) * 128, :]), 16)
                for ct in range(NCT):
                    lnx.inc(sy.dma_start(
                        out=krf[:, ct * 1024:(ct + 1) * 1024],
                        in_=kr_h.ap()[ct * 128:(ct + 1) * 128, :]), 16)
            block.sync(p1_sy)
            LNX_ALL = (NT + NCT) * 16

            def _p1_v2(v, st):
                sl = st % 4
                cb = sl * 8
                xs = xr[:, sl * 1024:(sl + 1) * 1024]
                mcol = lncols[:, cb + 2:cb + 3]
                rstd = lncols[:, cb + 6:cb + 7]
                mb = lncols[:, cb + 5:cb + 6]
                v.wait_ge(rst.sem, st + 1)
                if st >= 4:
                    v.wait_ge(xnw.sem, (st - 4 + 1) * 16)
                vsf.inc(v.tensor_scalar(mb, mcol, rstd, ccol[:, 5:6],
                                        ALU.mult, ALU.mult))
                v.wait_ge(vsf.sem, vsf.n)
                v2.inc(v.tensor_scalar(xnr[:, sl * 1024:(sl + 1) * 1024], xs,
                                       rstd, mb, ALU.mult, ALU.add))

            def p1_v(v):
                for st in range(NT):
                    sl = st % 4
                    xs = xr[:, sl * 1024:(sl + 1) * 1024]
                    cb = sl * 8
                    s1 = lncols[:, cb + 0:cb + 1]
                    s2 = lncols[:, cb + 1:cb + 2]
                    mcol = lncols[:, cb + 2:cb + 3]
                    msq = lncols[:, cb + 3:cb + 4]
                    varc = lncols[:, cb + 4:cb + 5]
                    v.wait_ge(lnx.sem, (st + 1) * 16)
                    if st >= 4:
                        v.wait_ge(rst.sem, (st - 4) + 1)
                    v.reduce_sum(s1, xs, axis=mybir.AxisListType.X)
                    v.scalar_tensor_tensor(junk[:], xs, ccol[:, 3:4], xs,
                                           ALU.mult, ALU.mult, accum_out=s2)
                    vsf.inc(v.tensor_scalar(mcol, s1, ccol[:, 1:2], None, ALU.mult))
                    v.wait_ge(vsf.sem, vsf.n)
                    v.tensor_scalar(msq, mcol, mcol, None, ALU.mult)
                    v1.inc(v.scalar_tensor_tensor(varc, s2, ccol[:, 1:2], msq,
                                                  ALU.mult, ALU.subtract))
                    if st >= 1:
                        _p1_v2(v, st - 1)
                _p1_v2(v, NT - 1)
                v.wait_ge(lnx.sem, LNX_ALL)
                for ct in range(NCT):
                    v.reduce_max(amx[:, ct:ct + 1],
                                 krf[:, ct * 1024:(ct + 1) * 1024],
                                 axis=mybir.AxisListType.X, apply_absolute_value=True)
                qnt.inc(v.reduce_max(amx[:, 8:9], amx[:, 0:8],
                                     axis=mybir.AxisListType.X))
                v.wait_ge(qnt.sem, 2)
                vsf.inc(v.reduce_max(sc1[0:1, 32:33], prod[0:1, 0:128],
                                     axis=mybir.AxisListType.X))
                v.wait_ge(vsf.sem, vsf.n)
                vsf.inc(v.reciprocal(sc1[0:1, 33:34], sc1[0:1, 32:33]))
                v.wait_ge(vsf.sem, vsf.n)
                v.tensor_scalar(sc1[0:1, 34:35], sc1[0:1, 33:34],
                                ccol[0:1, 7:8], None, ALU.mult)
                qnt.inc(v.tensor_scalar(sc1[0:1, 35:36], sc1[0:1, 32:33],
                                        ccol[0:1, 13:14], None, ALU.mult))
                v.wait_ge(qnt.sem, 35)
                for ct in range(NCT):
                    kf = krf[:, ct * 1024:(ct + 1) * 1024]
                    v.tensor_scalar(junk[:], kf, scol[:, 0:1], ccol[:, 8:9],
                                    ALU.mult, ALU.add)
                    v.tensor_scalar(junk[:], junk[:], ccol[:, 9:10], ccol[:, 7:8],
                                    ALU.add, ALU.min)
                    v.tensor_scalar(qint[:, ct * 1024:(ct + 1) * 1024], junk[:],
                                    ccol[:, 10:11], None, ALU.max)
                qnt.inc(v.nop())
                v.wait_ge(pre.sem, PRE_ALL)
                qnt.inc(v.tensor_scalar(baugb[:], baug[:], scol[0:17, 0:1], None,
                                        ALU.mult))
                v.wait_ge(aux.sem, 34)
                qnt.inc(v.tensor_scalar(hcol[:], hcol[:], ccol[:, 0:1], None,
                                        ALU.add))
                v.tensor_scalar(bqs[:], bqs[:], ccol[:, 6:7], None, ALU.mult)
            def p0_v2(v):
                v.wait_ge(aux.sem, 1)
                aux.inc(v.tensor_copy(strip_s[:], prod[0:16, 0:512]))
            block.vector(p0_v2)
            block.vector(p1_v)

            def p1_s(s):
                _preload_named_set(nc, s, "natural_log_exp_and_others")
                for st in range(NT):
                    sl = st % 4
                    cb = sl * 8
                    varc = lncols[:, cb + 4:cb + 5]
                    lnv = lncols[:, cb + 7:cb + 8]
                    rstd = lncols[:, cb + 6:cb + 7]
                    s.wait_ge(v1.sem, st + 1)
                    ssf.inc(s.activation(out=lnv, in_=varc, func=AF.Ln,
                                         bias=ccol[:, 2:3], scale=ccol[:, 3:4]))
                    s.wait_ge(ssf.sem, ssf.n)
                    rst.inc(s.activation(out=rstd, in_=lnv, func=AF.Exp,
                                         bias=ccol[:, 12:13], scale=ccol[:, 4:5]))
            block.scalar(p1_s)

            def p0_t(t):
                t.wait_ge(pre.sem, 32)
                aux.inc(t.matmul(prod[0:16, 0:512], relb_s[:], oh_s[:],
                                 start=True, stop=True))
                t.wait_ge(qnt.sem, 1)
                t.wait_ge(aux.sem, 2)
                qnt.inc(t.matmul(prod[0:1, 0:128], amx[:, 8:9],
                                 imat_s[:], start=True, stop=True))
            block.tensor(p0_t)

            def pg_mid(g):
                g.wait_ge(aux.sem, 2)
                aux.inc(g.dma_start(out=strip_hb.ap(), in_=strip_s[:]), 16)
                g.wait_ge(aux.sem, 18)
                aux.inc(g.dma_start(
                    out=hcol[:], in_=AP(strip_hb, 64, [[0, 128], [512, H]]),
                    allow_slow_non_contiguous=True), 16)
                for st in range(NT):
                    sl = st % 4
                    g.wait_ge(v2.sem, st + 1)
                    xnw.inc(g.dma_start(
                        out=xn_hb.ap()[st * 128:(st + 1) * 128, :],
                        in_=xnr[:, sl * 1024:(sl + 1) * 1024]), 16)
                g.wait_ge(qnt.sem, 3)
                g.dma_start(out=AP(sc_hb, 0, [[1, 1], [1, 2]]),
                            in_=sc1[0:1, 34:36]).then_inc(qnt.sem, 16)
                qnt.n += 16
                g.wait_ge(qnt.sem, 19)
                qnt.inc(g.dma_start(out=scol[:, 0:2],
                                    in_=AP(sc_hb, 0, [[0, 128], [1, 2]])), 16)
                if lvl < 4:
                    return
                for ct in range(NCT):
                    stld.inc(g.dma_start(
                        out=wbuf[:, ct * 1024:(ct + 1) * 1024],
                        in_=wq_h.ap()[ct * 128:(ct + 1) * 128, :]), 16)
                for k in range(N_FFN):
                    st, fc = k // 2, k % 2
                    gsl = k % 4
                    g.wait_ge(pev.sem, N_UT + k + 1)
                    pst.inc(g.dma_start(
                        out=g_hb.ap()[st, :, fc * 512:(fc + 1) * 512],
                        in_=g_ring[:, gsl * 512:(gsl + 1) * 512]), 16)
                if lvl < 5:
                    return
                for k in range(N_QK):
                    dt, sch = k // NQC, k % NQC
                    esl = k % 3
                    g.wait_ge(pev.sem, N_UT + N_FFN + k + 1)
                    pst.inc(g.dma_start(
                        out=qT_hb.ap()[dt, :, sch * 512:(sch + 1) * 512],
                        in_=evst[:, esl * 512:(esl + 1) * 512]), 16)
                g.wait_ge(prd.sem, N_UT + N_FFN + N_QK)
                for ct in range(NCT):
                    stld.inc(g.dma_start(
                        out=wbuf[:, ct * 1024:(ct + 1) * 1024],
                        in_=wk_h.ap()[ct * 128:(ct + 1) * 128, :]), 16)
                for k in range(N_QK):
                    dt, sch = k // NQC, k % NQC
                    esl = (N_QK + k) % 3
                    g.wait_ge(pev.sem, N_UT + N_FFN + N_QK + k + 1)
                    pst.inc(g.dma_start(
                        out=kT_hb.ap()[dt, :, sch * 512:(sch + 1) * 512],
                        in_=evst[:, esl * 512:(esl + 1) * 512]), 16)
                g.wait_ge(prd.sem, N_UT + N_FFN + 2 * N_QK)
                for ct in range(NCT):
                    stld.inc(g.dma_start(
                        out=wbuf[:, ct * 1024:(ct + 1) * 1024],
                        in_=wv_h.ap()[ct * 128:(ct + 1) * 128, :]), 16)
                for k in range(N_V):
                    st, dc = k // 2, k % 2
                    esl = (2 * N_QK + k) % 3
                    g.wait_ge(pev.sem, N_UT + N_FFN + 2 * N_QK + k + 1)
                    pst.inc(g.dma_start(
                        out=v_hb.ap()[2 * dc, st, :, :],
                        in_=evst[:, esl * 512:esl * 512 + 256]), 16)
                    pst.inc(g.dma_start(
                        out=v_hb.ap()[2 * dc + 1, st, :, :],
                        in_=evst[:, esl * 512 + 256:esl * 512 + 512]), 16)
            block.gpsimd(pg_mid)
            STLD_W = 3 * NCT * 16

            def p2_sy(sy):
                sy.wait_ge(xnw.sem, NT * 16)
                for ct in range(NCT):
                    xnt.inc(sy.dma_start_transpose(
                        xnT[:, ct * S:(ct + 1) * S],
                        xn_hb.ap()[:, ct * 128:(ct + 1) * 128]), 16)
            block.sync(p2_sy)
            XNT_ALL = NCT * 16

            def p34_t(t):
                t.wait_ge(xnt.sem, XNT_ALL)
                t.wait_ge(qnt.sem, 37)
                ch = 0
                if lvl < 4:
                    return
                for sch in range(NQC):
                    bk = (prod, ctxA, ctxB)[ch % 3]
                    if ch >= 3:
                        t.wait_ge(pev.sem, ch - 2)
                    mm = None
                    for ct in range(NCT):
                        mm = t.matmul(bk[0:16, :], a_s[:, ct * 16:(ct + 1) * 16],
                                      xnT[:, ct * S + sch * 512:ct * S + sch * 512 + 512],
                                      start=(ct == 0), stop=(ct == NCT - 1))
                    prd.inc(mm)
                    ch += 1
                for k in range(N_FFN):
                    st, fc = k // 2, k % 2
                    bk = (prod, ctxA, ctxB)[ch % 3]
                    if ch >= 3:
                        t.wait_ge(pev.sem, ch - 2)
                    for ct in range(NCT):
                        t.matmul(bk[:], xnT[:, ct * S + st * 128:ct * S + st * 128 + 128],
                                 qint[:, ct * 1024 + fc * 512:ct * 1024 + fc * 512 + 512],
                                 start=(ct == 0), stop=False)
                    mm = t.matmul(bk[:], uTa[:, st * 128:(st + 1) * 128],
                                  baugb[:, fc * 512:(fc + 1) * 512],
                                  start=False, stop=True)
                    prd.inc(mm)
                    ch += 1
                if lvl < 5:
                    return
                for wsel in range(2):
                    t.wait_ge(stld.sem, (wsel + 1) * NCT * 16)
                    for k in range(N_QK):
                        dt, sch = k // NQC, k % NQC
                        bk = (prod, ctxA, ctxB)[ch % 3]
                        t.wait_ge(pev.sem, ch - 2)
                        mm = None
                        for ct in range(NCT):
                            mm = t.matmul(
                                bk[:],
                                wbuf[:, ct * 1024 + dt * 128:ct * 1024 + dt * 128 + 128],
                                xnT[:, ct * S + sch * 512:ct * S + sch * 512 + 512],
                                start=(ct == 0), stop=(ct == NCT - 1))
                        prd.inc(mm)
                        ch += 1
                t.wait_ge(stld.sem, 3 * NCT * 16)
                for k in range(N_V):
                    st, dc = k // 2, k % 2
                    bk = (prod, ctxA, ctxB)[ch % 3]
                    t.wait_ge(pev.sem, ch - 2)
                    for ct in range(NCT):
                        t.matmul(bk[:], xnT[:, ct * S + st * 128:ct * S + st * 128 + 128],
                                 wbuf[:, ct * 1024 + dc * 512:ct * 1024 + dc * 512 + 512],
                                 start=(ct == 0), stop=False)
                    mm = t.matmul(bk[:], onesr[0:1, 0:128],
                                  bvrow[0:1, dc * 512:(dc + 1) * 512],
                                  start=False, stop=True)
                    prd.inc(mm)
                    ch += 1
            block.tensor(p34_t)

            def p34_v(v):
                ch = 0
                if lvl < 4:
                    return
                for sch in range(NQC):
                    bk = (prod, ctxA, ctxB)[ch % 3]
                    v.wait_ge(prd.sem, ch + 1)
                    pev.inc(v.tensor_copy(uTa[0:16, sch * 512:(sch + 1) * 512],
                                          bk[0:16, :]))
                    ch += 1
                for k in range(N_FFN):
                    bk = (prod, ctxA, ctxB)[ch % 3]
                    gsl = k % 4
                    v.wait_ge(rst.sem, NT + k + 1)
                    if k >= 4:
                        v.wait_ge(pst.sem, pst_after_store("g", k - 4))
                    pev.inc(v.scalar_tensor_tensor(
                        g_ring[:, gsl * 512:(gsl + 1) * 512], th_s[:],
                        ccol[:, 3:4], bk[:], ALU.add, ALU.mult))
                    ch += 1
                if lvl < 5:
                    return
                for wsel in range(2):
                    bias = bqs if wsel == 0 else bks
                    kind = "q" if wsel == 0 else "k"
                    for k in range(N_QK):
                        dt = k // NQC
                        q3 = wsel * N_QK + k
                        bk = (prod, ctxA, ctxB)[ch % 3]
                        esl = q3 % 3
                        v.wait_ge(prd.sem, ch + 1)
                        if q3 >= 3:
                            p3 = q3 - 3
                            v.wait_ge(pst.sem, pst_after_store(
                                "q" if p3 < N_QK else "k",
                                p3 if p3 < N_QK else p3 - N_QK))
                        if wsel == 0:
                            pev.inc(v.tensor_scalar(
                                evst[:, esl * 512:(esl + 1) * 512], bk[:],
                                ccol[:, 6:7], bias[:, dt:dt + 1],
                                ALU.mult, ALU.add))
                        else:
                            pev.inc(v.tensor_single_scalar(
                                evst[:, esl * 512:(esl + 1) * 512], bk[:],
                                bias[:, dt:dt + 1], ALU.add))
                        ch += 1
                for k in range(N_V):
                    q3 = 2 * N_QK + k
                    bk = (prod, ctxA, ctxB)[ch % 3]
                    esl = q3 % 3
                    v.wait_ge(prd.sem, ch + 1)
                    p3 = q3 - 3
                    v.wait_ge(pst.sem, pst_after_store(
                        "k" if p3 < 2 * N_QK else "v",
                        p3 - N_QK if p3 < 2 * N_QK else p3 - 2 * N_QK))
                    pev.inc(v.tensor_copy(evst[:, esl * 512:(esl + 1) * 512], bk[:]))
                    ch += 1
            block.vector(p34_v)

            def p4_s(s):
                if lvl < 4:
                    return
                s.preload_act_table(AF.Tanh)
                for k in range(N_FFN):
                    bk = (prod, ctxA, ctxB)[(N_UT + k) % 3]
                    s.wait_ge(prd.sem, N_UT + k + 1)
                    if k >= 1:
                        s.wait_ge(pev.sem, N_UT + k)
                    rst.inc(s.activation(out=th_s[:], in_=bk[:], func=AF.Tanh,
                                         bias=ccol[:, 12:13], scale=scol[:, 1:2]))
            block.scalar(p4_s)

        if lvl >= 6:
            es2 = contextlib.ExitStack()
            with es2:
                kTs = es2.enter_context(nc.sbuf_tensor("kTs", [128, 2 * 2 * S], BF16))
                v_s = es2.enter_context(nc.sbuf_tensor("v_s", [128, 2 * NT * 256], BF16))
                qsl = es2.enter_context(nc.sbuf_tensor("qsl", [128, 2 * 2 * 512], BF16))
                attnU = es2.enter_context(nc.sbuf_tensor("attnU", [128, 8 * 1024], BF16))
                btile = es2.enter_context(nc.sbuf_tensor("btile", [128, 2 * 8 * 128], BF16))
                sTst = es2.enter_context(nc.sbuf_tensor("sTst", [128, 4 * 512], BF16))
                dnst = es2.enter_context(nc.sbuf_tensor("dnst", [128, 2 * 512], F32))
                e_s = es2.enter_context(nc.sbuf_tensor("e_s", [128, 2 * 1024], BF16))
                e_g = es2.enter_context(nc.sbuf_tensor("e_g", [128, 2 * 1024], BF16))
                e_x = es2.enter_context(nc.sbuf_tensor("e_x", [128, 2 * 1024], F32))
                e_o = es2.enter_context(nc.sbuf_tensor("e_o", [128, 2 * 1024], F32))
                e_r = es2.enter_context(nc.sbuf_tensor("e_r", [128, 2 * 16], F32))
                denT = es2.enter_context(nc.sbuf_tensor("denT", [128, 512], F32))
                NW = 4
                W_PER_QC = 16 * NW

                def w_index(hq, qc, kp, hi):
                    return ((hq * NQC + qc) * 16 + kp) * NW + hi

                STLD_HQ = (2 + NT) * 16

                def p6_sy(sy):
                    sy.wait_ge(pst.sem, PST_ALL)
                    for hq in range(4):
                        sset = hq % 2
                        if hq >= 2:
                            sy.wait_ge(cns.sem, (hq - 1) * NQC * 16)
                        for dt2 in range(2):
                            stld.inc(sy.dma_start(
                                out=kTs[:, (sset * 2 + dt2) * S:(sset * 2 + dt2 + 1) * S],
                                in_=kT_hb.ap()[2 * hq + dt2, :, :]), 16)
                        for kt in range(NT):
                            stld.inc(sy.dma_start(
                                out=v_s[:, sset * NT * 256 + kt * 256:
                                        sset * NT * 256 + (kt + 1) * 256],
                                in_=v_hb.ap()[hq, kt, :, :]), 16)
                        for qc in range(NQC):
                            gqc = hq * NQC + qc
                            qset = gqc % 2
                            if gqc >= 2:
                                sy.wait_ge(sc.sem, (gqc - 1) * W_PER_QC)
                            for dt2 in range(2):
                                qld.inc(sy.dma_start(
                                    out=qsl[:, (qset * 2 + dt2) * 512:
                                            (qset * 2 + dt2 + 1) * 512],
                                    in_=qT_hb.ap()[2 * hq + dt2, :,
                                                   qc * 512:(qc + 1) * 512]), 16)
                block.sync(p6_sy)

                def p6_g(g):
                    g.wait_ge(aux.sem, 18)
                    for hq in range(4):
                        bset = hq % 2
                        if hq >= 2:
                            g.wait_ge(cns.sem, (hq - 1) * NQC * 16)
                        for hi in range(4):
                            h = 4 * hq + hi
                            for dt in range(2):
                                j0 = 128 * dt + 1
                                btl.inc(g.dma_start(
                                    out=btile[:, (bset * 8 + hi * 2 + dt) * 128:
                                              (bset * 8 + hi * 2 + dt + 1) * 128],
                                    in_=AP(strip_hb, h * 512 + j0, [[1, 128], [1, 128]])), 16)
                        for qc in range(NQC):
                            gqc = hq * NQC + qc
                            esl = gqc % 2
                            hp = 2 * hq
                            g.wait_ge(epi.sem, gqc * 3 + 2)
                            est.inc(g.dma_start(
                                out=sT_hb.ap()[hp * 128:(hp + 1) * 128,
                                               qc * 512:(qc + 1) * 512],
                                in_=sTst[:, esl * 1024:esl * 1024 + 512]), 16)
                            est.inc(g.dma_start(
                                out=sT_hb.ap()[(hp + 1) * 128:(hp + 2) * 128,
                                               qc * 512:(qc + 1) * 512],
                                in_=sTst[:, esl * 1024 + 512:esl * 1024 + 1024]), 16)
                            g.wait_ge(epi.sem, gqc * 3 + 3)
                            est.inc(g.dma_start(
                                out=den_hb.ap()[4 * hq:4 * hq + 4, qc * 512:(qc + 1) * 512],
                                in_=AP(dnst, esl * 512, [[32 * 1024, 4], [1, 512]])), 16)
                if dump == "w0":
                    def pdbg_v(v):
                        v.wait_ge(ex.sem, 1)
                        v.tensor_copy(e_x[:, 0:1024], ring[:, 0:1024])
                        dbgw.inc(v.tensor_copy(e_o[:, 0:1024], attnU[:, 0:1024]))
                    block.vector(pdbg_v)

                    def pdbg_g(g):
                        g.wait_ge(dbgw.sem, 1)
                        dbgw.inc(g.dma_start(out=dbg2_h.ap()[:, 0:1024],
                                             in_=e_x[:, 0:1024]), 16)
                        dbgw.inc(g.dma_start(out=dbg2_h.ap()[:, 1024:2048],
                                             in_=e_o[:, 0:1024]), 16)
                    block.gpsimd(pdbg_g)

                block.gpsimd(p6_g)


                def _p6_cons(t, hq, qc, kp, sset):
                    gqc = hq * NQC + qc
                    kpg = gqc * 16 + kp
                    t.wait_ge(ex.sem, kpg * NW + NW)
                    if kp == 0 and gqc >= 1:
                        t.wait_ge(epi.sem, gqc * 3)
                    first = (kp == 0)
                    lastg = (kp == 15)
                    last = None
                    for j in range(2):
                        kt = 2 * kp + j
                        for pi, ctx in ((0, ctxA), (1, ctxB)):
                            for ci in range(2):
                                hi = pi * 2 + ci
                                w = w_index(hq, qc, kp, hi)
                                au = attnU[:, (w % 8) * 1024 + j * 512:
                                           (w % 8) * 1024 + j * 512 + 512]
                                t.matmul(ctx[ci * 64:(ci + 1) * 64, :],
                                         v_s[:, sset * NT * 256 + kt * 256 + hi * 64:
                                             sset * NT * 256 + kt * 256 + hi * 64 + 64],
                                         au,
                                         start=(first and j == 0),
                                         stop=(lastg and j == 1),
                                         tile_position=(0, ci * 64))
                        for hi in range(4):
                            w = w_index(hq, qc, kp, hi)
                            au = attnU[:, (w % 8) * 1024 + j * 512:
                                       (w % 8) * 1024 + j * 512 + 512]
                            last = t.matmul(denp[32 * hi:32 * (hi + 1), :], twos_s[:], au,
                                            start=(first and j == 0),
                                            stop=(lastg and j == 1),
                                            tile_position=(0, 32 * hi))
                    cns.inc(last)

                def p6_t(t):
                    for hq in range(4):
                        sset = hq % 2
                        bset = hq % 2
                        t.wait_ge(stld.sem, STLD_W + (hq + 1) * STLD_HQ)
                        t.wait_ge(btl.sem, (hq + 1) * 8 * 16)
                        for qc in range(NQC):
                            gqc = hq * NQC + qc
                            qset = gqc % 2
                            t.wait_ge(qld.sem, (gqc + 1) * 2 * 16)
                            for kp in range(16):
                                for hi in range(4):
                                    w = w_index(hq, qc, kp, hi)
                                    slot = w % 2
                                    if w >= 2:
                                        t.wait_ge(ex.sem, w - 1)
                                    if dump == "w0" and w == 2:
                                        t.wait_ge(dbgw.sem, 1)
                                    last = None
                                    for j in range(2):
                                        kt = 2 * kp + j
                                        boff = slot * 1024 + j * 512
                                        dtl = sset * 2 + hi // 2
                                        rb = (hi % 2) * 64
                                        cls = _kt_class(kt, qc)
                                        last = t.matmul(
                                            ring[:, boff:boff + 512],
                                            kTs[rb:rb + 64, dtl * S + kt * 128:
                                                dtl * S + kt * 128 + 128],
                                            qsl[rb:rb + 64,
                                                (qset * 2 + hi // 2) * 512:
                                                (qset * 2 + hi // 2 + 1) * 512],
                                            start=True, stop=(cls != "diag"),
                                            tile_position=((hi % 2) * 64, 0))
                                        if cls == "diag":
                                            subs = [qt - 4 * qc for qt in (kt, kt + 1)
                                                    if 4 * qc <= qt <= 4 * qc + 3]
                                            for idx, sub in enumerate(subs):
                                                dt = (sub + 4 * qc) - kt
                                                last = t.matmul(
                                                    ring[:, boff + sub * 128:
                                                         boff + sub * 128 + 128],
                                                    jmat_s[:],
                                                    btile[:, (bset * 8 + hi * 2 + dt) * 128:
                                                          (bset * 8 + hi * 2 + dt + 1) * 128],
                                                    start=False,
                                                    stop=(idx == len(subs) - 1))
                                    sc.inc(last)
                                if kp >= 1:
                                    _p6_cons(t, hq, qc, kp - 1, sset)
                            _p6_cons(t, hq, qc, 15, sset)
                            t.wait_ge(epi.sem, gqc * 3 + 3)
                            if gqc >= 1:
                                t.wait_ge(dtx.sem, 2 * gqc)
                            mm = None
                            for i in range(4):
                                mm = t.matmul(
                                    prod[:, i * 128:(i + 1) * 128],
                                    dnst[:, (gqc % 2) * 512 + i * 128:
                                         (gqc % 2) * 512 + (i + 1) * 128],
                                    imat_s[:], start=True, stop=True)
                            dtx.inc(mm)
                block.tensor(p6_t)

                def p6_s(s):
                    for hq in range(4):
                        for qc in range(NQC):
                            for kp in range(16):
                                for hi in range(4):
                                    h = 4 * hq + hi
                                    w = w_index(hq, qc, kp, hi)
                                    slot = w % 2
                                    s.wait_ge(sc.sem, w + 1)
                                    if w >= 8:
                                        s.wait_ge(cns.sem, (w // NW) - 1)
                                    cls = [_kt_class(2 * kp + j, qc) for j in range(2)]
                                    def bias_of(c):
                                        return hcol[:, h:h + 1] if c == "fut" else ccol[:, 0:1]
                                    dsl = (w % 8) * 1024
                                    if (cls[0] == "fut") == (cls[1] == "fut"):
                                        last = s.activation(
                                            out=attnU[:, dsl:dsl + 1024],
                                            in_=ring[:, slot * 1024:slot * 1024 + 1024],
                                            func=AF.Exp, bias=bias_of(cls[0]))
                                    else:
                                        s.activation(
                                            out=attnU[:, dsl:dsl + 512],
                                            in_=ring[:, slot * 1024:slot * 1024 + 512],
                                            func=AF.Exp, bias=bias_of(cls[0]))
                                        last = s.activation(
                                            out=attnU[:, dsl + 512:dsl + 1024],
                                            in_=ring[:, slot * 1024 + 512:slot * 1024 + 1024],
                                            func=AF.Exp, bias=bias_of(cls[1]))
                                    ex.inc(last)
                block.scalar(p6_s)

                def p6_v(v):
                    for hq in range(4):
                        for qc in range(NQC):
                            gqc = hq * NQC + qc
                            esl = gqc % 2
                            v.wait_ge(cns.sem, (gqc + 1) * 16)
                            if gqc >= 2:
                                v.wait_ge(est.sem, (gqc - 1) * 2 * 16)
                            epi.inc(v.tensor_copy(sTst[:, esl * 1024:esl * 1024 + 512],
                                                  ctxA[:]))
                            epi.inc(v.tensor_copy(sTst[:, esl * 1024 + 512:
                                                        esl * 1024 + 1024], ctxB[:]))
                            epi.inc(v.tensor_scalar(dnst[:, esl * 512:(esl + 1) * 512],
                                                    denp[:], scol[:, 0:1], None, ALU.mult))
                            v.wait_ge(dtx.sem, 2 * gqc + 1)
                            dtx.inc(v.tensor_copy(
                                AP(denT, qc * 64 + 4 * hq, [[512, 128], [16, 4], [1, 4]]),
                                AP(prod, 0, [[512, 128], [128, 4], [32, 4]])))
                    for qt in range(NT):
                        esl = qt % 2
                        v.wait_ge(eld.sem, (qt + 1) * 3 * 16)
                        if qt >= 2:
                            v.wait_ge(ost.sem, (qt - 1) * 16)
                        vsf.inc(v.reciprocal(e_r[:, esl * 16:(esl + 1) * 16],
                                             denT[:, qt * 16:(qt + 1) * 16]))
                        v.wait_ge(vsf.sem, vsf.n)
                        rE = AP(e_r, esl * 16, [[32, 128], [1, 16], [0, 64]])
                        if dump == "edn":
                            ecp.inc(v.tensor_copy(
                                e_o[:, esl * 1024:esl * 1024 + 512], denT[:]))
                            continue
                        if dump == "es":
                            ecp.inc(v.tensor_copy(
                                e_o[:, esl * 1024:(esl + 1) * 1024],
                                e_s[:, esl * 1024:(esl + 1) * 1024]))
                            continue
                        if dump == "eg":
                            ecp.inc(v.tensor_copy(
                                e_o[:, esl * 1024:(esl + 1) * 1024],
                                e_g[:, esl * 1024:(esl + 1) * 1024]))
                            continue
                        if dump == "er":
                            ecp.inc(v.tensor_copy(
                                e_o[:, esl * 1024:(esl + 1) * 1024], rE))
                            continue
                        if dump == "ex":
                            ecp.inc(v.tensor_copy(
                                e_o[:, esl * 1024:(esl + 1) * 1024],
                                e_x[:, esl * 1024:(esl + 1) * 1024]))
                            continue
                        v.tensor_tensor(e_o[:, esl * 1024:(esl + 1) * 1024],
                                        e_s[:, esl * 1024:(esl + 1) * 1024], rE, ALU.mult)
                        v.tensor_tensor(e_o[:, esl * 1024:(esl + 1) * 1024],
                                        e_o[:, esl * 1024:(esl + 1) * 1024],
                                        e_g[:, esl * 1024:(esl + 1) * 1024], ALU.mult)
                        ecp.inc(v.tensor_tensor(e_o[:, esl * 1024:(esl + 1) * 1024],
                                                e_o[:, esl * 1024:(esl + 1) * 1024],
                                                e_x[:, esl * 1024:(esl + 1) * 1024],
                                                ALU.add))
                block.vector(p6_v)

                def p7_sy(sy):
                    for qt in range(NT):
                        esl = qt % 2
                        qcq = qt // 4
                        sy.wait_ge(est.sem, (3 * NQC + qcq + 1) * 2 * 16)
                        if qt >= 2:
                            sy.wait_ge(ecp.sem, qt - 1)
                        eld.inc(sy.dma_start_transpose(
                            e_s[:, esl * 1024:(esl + 1) * 1024],
                            sT_hb.ap()[:, qt * 128:(qt + 1) * 128]), 16)
                        eld.inc(sy.dma_start(out=e_g[:, esl * 1024:(esl + 1) * 1024],
                                             in_=g_hb.ap()[qt, :, :]), 16)
                        eld.inc(sy.dma_start(out=e_x[:, esl * 1024:(esl + 1) * 1024],
                                             in_=x_h.ap()[qt * 128:(qt + 1) * 128, :]), 16)
                    sy.wait_ge(ost.sem, NT * 16)
                block.sync(p7_sy)

                def p7_g(g):
                    for qt in range(NT):
                        esl = qt % 2
                        g.wait_ge(ecp.sem, qt + 1)
                        ost.inc(g.dma_start(out=out_h.ap()[qt * 128:(qt + 1) * 128, :],
                                            in_=e_o[:, esl * 1024:(esl + 1) * 1024]), 16)
                block.gpsimd(p7_g)

        if lvl < 7:
            _sdmp = es.enter_context(nc.semaphore("s_dump"))
            dmp = Ctr(_sdmp)

            def pdump_g(g):
                g.wait_ge(qnt.sem, 38)
                dmp.inc(g.dma_start(out=dbg_h.ap()[:, 0:32], in_=lncols[:]), 16)
                dmp.inc(g.dma_start(out=dbg_h.ap()[:, 32:36], in_=scol[:]), 16)
                dmp.inc(g.dma_start(out=dbg_h.ap()[0:1, 36:52],
                                    in_=sc1[0:1, 28:44]), 16)
                if dump == "xn":
                    g.wait_ge(xnw.sem, NT * 16)
                    g.wait_ge(xnt.sem, XNT_ALL)
                    for st in range(NT):
                        dmp.inc(g.dma_start(
                            out=out_h.ap()[st * 128:(st + 1) * 128, :],
                            in_=xn_hb.ap()[st * 128:(st + 1) * 128, :]), 16)
                elif dump == "qint":
                    g.wait_ge(qnt.sem, 36)
                    for ct in range(NCT):
                        dmp.inc(g.dma_start(
                            out=out_h.ap()[ct * 128:(ct + 1) * 128, :],
                            in_=qint[:, ct * 1024:(ct + 1) * 1024]), 16)
                elif dump == "g":
                    g.wait_ge(pst.sem, N_FFN * 16)
                    dmp.inc(g.dma_start(
                        out=AP(out_h, 0, [[Dm, NT * 128], [1, Dm]]),
                        in_=g_hb.ap().rearrange("a p b -> (a p) b")), 16)
                elif dump == "qT":
                    g.wait_ge(pst.sem, (N_FFN + N_QK) * 16)
                    dmp.inc(g.dma_start(
                        out=AP(out_h, 0, [[S, NCT * 128], [1, S]]),
                        in_=qT_hb.ap().rearrange("a p b -> (a p) b")), 16)
                elif dump == "kT":
                    g.wait_ge(pst.sem, (N_FFN + 2 * N_QK) * 16)
                    dmp.inc(g.dma_start(
                        out=AP(out_h, 0, [[S, NCT * 128], [1, S]]),
                        in_=kT_hb.ap().rearrange("a p b -> (a p) b")), 16)
                elif dump == "v":
                    g.wait_ge(pst.sem, PST_ALL)
                    dmp.inc(g.dma_start(
                        out=AP(out_h, 0, [[256, 4 * NT * 128], [1, 256]]),
                        in_=v_hb.ap().rearrange("a b p c -> (a b p) c")), 16)
                elif dump == "sT":
                    g.wait_ge(est.sem, est.n)
                    dmp.inc(g.dma_start(
                        out=AP(out_h, 0, [[S, Dm], [1, S]]),
                        in_=sT_hb.ap()), 16)
                elif dump == "den":
                    g.wait_ge(est.sem, est.n)
                    dmp.inc(g.dma_start(
                        out=AP(out_h, 0, [[S, H], [1, S]]),
                        in_=den_hb.ap()), 16)
            block.gpsimd(pdump_g)

            def pdump_sy(sy):
                if dmp.n:
                    sy.wait_ge(dmp.sem, dmp.n)
                if dump == "w0":
                    sy.wait_ge(dbgw.sem, 33)
            block.sync(pdump_sy)

    return nc


_NC_CACHE = None


def kernel(**inputs):
    global _NC_CACHE
    oh, jmat, imat = _host_consts()
    x = np.ascontiguousarray(inputs["L0_attn_hidden_states"], dtype=np.float32)
    base = {
        "wq": np.ascontiguousarray(inputs["L0_attn_Wq"], dtype=np.float32),
        "bq": np.ascontiguousarray(inputs["L0_attn_bq"], dtype=np.float32),
        "wk": np.ascontiguousarray(inputs["L0_attn_Wk"], dtype=np.float32),
        "bk": np.ascontiguousarray(inputs["L0_attn_bk"], dtype=np.float32),
        "wv": np.ascontiguousarray(inputs["L0_attn_Wv"], dtype=np.float32),
        "bv": np.ascontiguousarray(inputs["L0_attn_bv"], dtype=np.float32),
        "relb": np.ascontiguousarray(inputs["L0_attn_rel_bias"], dtype=np.float32),
        "ker": np.ascontiguousarray(inputs["L0_ffn_kernel"], dtype=np.float32),
        "fbias": np.ascontiguousarray(inputs["L0_ffn_bias"], dtype=np.float32),
        "a_w": np.ascontiguousarray(inputs["L0_ffn_a"], dtype=np.float32),
        "b_w": np.ascontiguousarray(inputs["L0_ffn_b"], dtype=np.float32),
        "onehot": oh, "jmat": jmat, "imat": imat,
    }
    import os
    lvl = int(os.environ.get("KLVL", "7"))
    dump = os.environ.get("KDUMP") or None
    if _NC_CACHE is None:
        _NC_CACHE = build_nc(lvl, dump)
    nc = _NC_CACHE
    in_maps = [dict(base, x=np.ascontiguousarray(x[b])) for b in range(8)]
    res = run_bass_kernel_spmd(nc, in_maps, core_ids=list(range(8)))
    return np.stack([res.results[b]["out"] for b in range(8)], axis=0)


# version 17
# speedup vs baseline: 1.2534x; 1.2534x over previous
import math
import numpy as np
import ml_dtypes
import concourse.bass as bass
import concourse.mybir as mybir
from concourse.bass import AP
from concourse.bass_utils import run_bass_kernel_spmd

F32 = mybir.dt.float32
BF16 = mybir.dt.bfloat16
FP16 = mybir.dt.float16
AF = mybir.ActivationFunctionType
ALU = mybir.AluOpType

S = 4096
Dm = 1024
H = 16
NT = 32
NCT = 8
NQC = 8
CEXP = 8.0


def _bucket(delta):
    n = max(delta, 0)
    if n == 0:
        return 0
    if n < 16:
        return n
    return min(16 + int(math.log(n / 16.0) / math.log(2.0) * 16.0), 31)


def _host_consts():
    oh = np.zeros((32, 512), dtype=np.float32)
    for j in range(512):
        oh[_bucket(j - 128), j] += 1.0
        oh[31, j] -= 1.0
    jmat = np.zeros((128, 128), dtype=np.float32)
    for i in range(128):
        jmat[i, 127 - i] = 1.0
    imat = np.eye(128, dtype=np.float32)
    return oh, jmat, imat


class Ctr:
    def __init__(self, sem):
        self.sem = sem
        self.n = 0

    def inc(self, inst, k=1):
        inst.then_inc(self.sem, k)
        self.n += k
        return self.n


def _kt_class(kt, qc):
    qlo, qhi = 4 * qc, 4 * qc + 3
    if kt > qhi:
        return "fut"
    if kt >= qlo - 1:
        return "diag"
    return "past"


def _preload_named_set(nc, eng, name):
    from concourse.hw_specs import get_activation_tables
    for idx, nm in enumerate(get_activation_tables(nc.m.arch)):
        if nm == name:
            return eng.add_instruction(mybir.InstLoadActFuncSet(
                name=nc.get_next_instruction_name(), act_func_set_id=idx,
                ins=[], outs=[]))
    raise ValueError(name)


def build_nc(lvl=7, dump=None):
    nc = bass.Bass("TRN2", target_bir_lowering=False)

    x_h = nc.dram_tensor("x", [S, Dm], F32, kind="ExternalInput")
    wq_h = nc.dram_tensor("wq", [Dm, Dm], F32, kind="ExternalInput")
    bq_h = nc.dram_tensor("bq", [Dm], F32, kind="ExternalInput")
    wk_h = nc.dram_tensor("wk", [Dm, Dm], F32, kind="ExternalInput")
    bk_h = nc.dram_tensor("bk", [Dm], F32, kind="ExternalInput")
    wv_h = nc.dram_tensor("wv", [Dm, Dm], F32, kind="ExternalInput")
    bv_h = nc.dram_tensor("bv", [Dm], F32, kind="ExternalInput")
    rb_h = nc.dram_tensor("relb", [32, H], F32, kind="ExternalInput")
    kr_h = nc.dram_tensor("ker", [Dm, Dm], F32, kind="ExternalInput")
    fb_h = nc.dram_tensor("fbias", [Dm], F32, kind="ExternalInput")
    a_h = nc.dram_tensor("a_w", [Dm, 16], F32, kind="ExternalInput")
    b_h = nc.dram_tensor("b_w", [16, Dm], F32, kind="ExternalInput")
    oh_h = nc.dram_tensor("onehot", [32, 512], F32, kind="ExternalInput")
    jm_h = nc.dram_tensor("jmat", [128, 128], F32, kind="ExternalInput")
    im_h = nc.dram_tensor("imat", [128, 128], F32, kind="ExternalInput")
    out_h = nc.dram_tensor("out", [S, Dm], F32, kind="ExternalOutput")
    dbg_h = nc.dram_tensor("dbg", [128, 64], F32, kind="ExternalOutput")
    dbg2_h = nc.dram_tensor("dbg2", [128, 4096], F32, kind="ExternalOutput")

    xn_hb = nc.dram_tensor("xn_hb", [S, Dm], BF16, kind="Internal")
    strip_hb = nc.dram_tensor("strip_hb", [H, 512], F32, kind="Internal")
    qT_hb = nc.dram_tensor("qT_hb", [NCT, 128, S], BF16, kind="Internal")
    kT_hb = nc.dram_tensor("kT_hb", [NCT, 128, S], BF16, kind="Internal")
    v_hb = nc.dram_tensor("v_hb", [4, NT, 128, 256], BF16, kind="Internal")
    g_hb = nc.dram_tensor("g_hb", [NT, 128, Dm], BF16, kind="Internal")
    sT_hb = nc.dram_tensor("sT_hb", [Dm, S], BF16, kind="Internal")
    den_hb = nc.dram_tensor("den_hb", [H, S], FP16, kind="Internal")
    sc_hb = nc.dram_tensor("sc_hb", [4], F32, kind="Internal")

    N_UT, N_FFN, N_QK, N_V = NQC, NT * 2, NCT * NQC, NT * 2
    N_CH = N_UT + N_FFN + 2 * N_QK + N_V

    def pst_after_store(idx_kind, k):
        if idx_kind == "g":
            return (k + 1) * 16
        if idx_kind == "q":
            return (N_FFN + k + 1) * 16
        if idx_kind == "k":
            return (N_FFN + N_QK + k + 1) * 16
        return (N_FFN + 2 * N_QK + 2 * (k + 1)) * 16

    PST_ALL = (N_FFN + 2 * N_QK + 2 * N_V) * 16

    import contextlib
    es = contextlib.ExitStack()
    with es:
        ring = es.enter_context(nc.psum_tensor("ring", [128, 3072], F32))
        ctxA = es.enter_context(nc.psum_tensor("ctxA", [128, 512], F32))
        ctxB = es.enter_context(nc.psum_tensor("ctxB", [128, 512], F32))
        ccol = es.enter_context(nc.sbuf_tensor("ccol", [128, 16], F32))
        hcol = es.enter_context(nc.sbuf_tensor("hcol", [128, H], F32))
        bqs = es.enter_context(nc.sbuf_tensor("bqs", [128, NCT], F32))
        bks = es.enter_context(nc.sbuf_tensor("bks", [128, NCT], F32))
        bvrow = es.enter_context(nc.sbuf_tensor("bvrow", [1, Dm], BF16))
        relb_s = es.enter_context(nc.sbuf_tensor("relb_s", [32, H], F32))
        oh_s = es.enter_context(nc.sbuf_tensor("oh_s", [32, 512], F32))
        strip_s = es.enter_context(nc.sbuf_tensor("strip_s", [16, 512], F32))
        jmat_s = es.enter_context(nc.sbuf_tensor("jmat_s", [128, 128], BF16))
        imat_s = es.enter_context(nc.sbuf_tensor("imat_s", [128, 128], F32))
        imat_rs = es.enter_context(nc.sbuf_tensor("imat_rs", [128, 128], F32))
        twos_s = es.enter_context(nc.sbuf_tensor("twos_s", [128, 32], BF16))
        onesr = es.enter_context(nc.sbuf_tensor("onesr", [1, 512], BF16))
        scol = es.enter_context(nc.sbuf_tensor("scol", [128, 4], F32))
        sc1 = es.enter_context(nc.sbuf_tensor("sc1", [1, 64], F32))
        sems = [es.enter_context(nc.semaphore("s_%d" % i)) for i in range(29)]
        (_s0, _s1, _s2, _s3, _s4, _s5, _s6, _s7, _s8, _s9, _s10, _s11, _s12,
         _s13, _s14, _s15, _s16, _s17, _s18, _s19, _s20, _s21, _s22,
         _s23, _s24, _s25, _s26, _s27, _s28) = sems
        block = es.enter_context(nc.Block())
        pre, lnx, v1, rst, v2 = Ctr(_s0), Ctr(_s1), Ctr(_s2), Ctr(_s3), Ctr(_s4)
        xnw, xnt, qnt, prd, pev = Ctr(_s5), Ctr(_s6), Ctr(_s7), Ctr(_s8), Ctr(_s9)
        pst, stld, qld, sc, ex = Ctr(_s10), Ctr(_s11), Ctr(_s12), Ctr(_s13), Ctr(_s14)
        cns, epi, est, eld, ecp = Ctr(_s15), Ctr(_s16), Ctr(_s17), Ctr(_s18), Ctr(_s19)
        ost, aux, btl = Ctr(_s20), Ctr(_s21), Ctr(_s22)
        vsf, ssf = Ctr(_s23), Ctr(_s24)
        dbgw = Ctr(_s25)
        dtx = Ctr(_s26)
        dnu, dvd = Ctr(_s27), Ctr(_s28)


        es1 = contextlib.ExitStack()
        with es1:
            xnT = es1.enter_context(nc.sbuf_tensor("xnT", [128, NCT * S], BF16))
            qint = es1.enter_context(nc.sbuf_tensor("qint", [128, NCT * 1024], BF16))
            wbuf = es1.enter_context(nc.sbuf_tensor("wbuf", [128, NCT * 1024], BF16))
            krf = es1.enter_context(nc.sbuf_tensor("krf", [128, NCT * 1024], F32))
            xr = es1.enter_context(nc.sbuf_tensor("xr", [128, 4 * 1024], F32))
            xnr = es1.enter_context(nc.sbuf_tensor("xnr", [128, 4 * 1024], BF16))
            junk = es1.enter_context(nc.sbuf_tensor("junk", [128, 1024], F32))
            lncols = es1.enter_context(nc.sbuf_tensor("lncols", [128, 4 * 8], F32))
            amx = es1.enter_context(nc.sbuf_tensor("amx", [128, 10], F32))
            evst = es1.enter_context(nc.sbuf_tensor("evst", [128, 3 * 512], BF16))
            baug = es1.enter_context(nc.sbuf_tensor("baug", [17, 1024], F32))
            baugb = es1.enter_context(nc.sbuf_tensor("baugb", [17, 1024], BF16))
            uTa = es1.enter_context(nc.sbuf_tensor("uTa", [17, S], BF16))
            a_s = es1.enter_context(nc.sbuf_tensor("a_s", [128, NCT * 16], BF16))
            th_s = es1.enter_context(nc.sbuf_tensor("th_s", [128, 512], F32))
            g_ring = es1.enter_context(nc.sbuf_tensor("g_ring", [128, 4 * 512], BF16))
            def p0_g(g):
                pre.inc(g.dma_start(out=relb_s[:], in_=rb_h.ap()), 16)
                pre.inc(g.dma_start(out=oh_s[:], in_=oh_h.ap()), 16)
                pre.inc(g.dma_start(out=jmat_s[:], in_=jm_h.ap()), 16)
                pre.inc(g.dma_start(out=imat_s[:], in_=im_h.ap()), 16)
                pre.inc(g.dma_start(out=bqs[:],
                                    in_=AP(bq_h, 0, [[1, 128], [128, NCT]]),
                                    allow_slow_non_contiguous=True), 16)
                pre.inc(g.dma_start(out=bks[:],
                                    in_=AP(bk_h, 0, [[1, 128], [128, NCT]]),
                                    allow_slow_non_contiguous=True), 16)
                pre.inc(g.dma_start(out=bvrow[:],
                                    in_=AP(bv_h, 0, [[0, 1], [1, Dm]])), 16)
                pre.inc(g.dma_start(
                    out=a_s[:], in_=AP(a_h, 0, [[16, 128], [2048, NCT], [1, 16]])), 16)
                pre.inc(g.dma_start(out=baug[0:16, :], in_=b_h.ap()), 16)
                pre.inc(g.dma_start(out=baug[16:17, :],
                                    in_=AP(fb_h, 0, [[0, 1], [1, Dm]])), 16)
                g.wait_ge(pre.sem, 161)
                pre.inc(g.dma_start(
                    out=uTa[16:17, :],
                    in_=AP(onesr, 0, [[512, 1], [0, 8], [1, 512]])), 16)
            block.gpsimd(p0_g)
            PRE_LOADS = 160

            def p0_v(v):
                v.memset(twos_s[:], 2.0)
                v.memset(onesr[:], 1.0)
                v.memset(ccol[:, 0:1], -CEXP)
                v.memset(ccol[:, 1:2], 1.0 / Dm)
                v.memset(ccol[:, 2:3], 1e-5)
                v.memset(ccol[:, 3:4], 1.0)
                v.memset(ccol[:, 4:5], -0.5)
                v.memset(ccol[:, 5:6], -1.0)
                v.memset(ccol[:, 6:7], 0.125)
                v.memset(ccol[:, 7:8], 7.0)
                v.memset(ccol[:, 8:9], 12582912.0)
                v.memset(ccol[:, 9:10], -12582912.0)
                v.memset(ccol[:, 10:11], -8.0)
                v.memset(ccol[:, 11:12], 0.5)
                v.memset(ccol[:, 12:13], 0.0)
                v.memset(ccol[:, 13:14], 1.0 / 14.0)
                pre.inc(v.memset(ccol[:, 14:15], 0.0))
            block.vector(p0_v)
            PRE_ALL = PRE_LOADS + 1 + 16

            def p1_sy(sy):
                for st in range(NT):
                    if st >= 4:
                        sy.wait_ge(v2.sem, (st - 4) + 1)
                    lnx.inc(sy.dma_start(
                        out=xr[:, (st % 4) * 1024:(st % 4 + 1) * 1024],
                        in_=x_h.ap()[st * 128:(st + 1) * 128, :]), 16)
                for ct in range(NCT):
                    lnx.inc(sy.dma_start(
                        out=krf[:, ct * 1024:(ct + 1) * 1024],
                        in_=kr_h.ap()[ct * 128:(ct + 1) * 128, :]), 16)
            block.sync(p1_sy)
            LNX_ALL = (NT + NCT) * 16

            def _p1_v2(v, st):
                sl = st % 4
                cb = sl * 8
                xs = xr[:, sl * 1024:(sl + 1) * 1024]
                mcol = lncols[:, cb + 2:cb + 3]
                rstd = lncols[:, cb + 6:cb + 7]
                mb = lncols[:, cb + 5:cb + 6]
                v.wait_ge(rst.sem, st + 1)
                if st >= 4:
                    v.wait_ge(xnw.sem, (st - 4 + 1) * 16)
                vsf.inc(v.tensor_scalar(mb, mcol, rstd, ccol[:, 5:6],
                                        ALU.mult, ALU.mult))
                v.wait_ge(vsf.sem, vsf.n)
                v2.inc(v.tensor_scalar(xnr[:, sl * 1024:(sl + 1) * 1024], xs,
                                       rstd, mb, ALU.mult, ALU.add))

            def p1_v(v):
                for st in range(NT):
                    sl = st % 4
                    xs = xr[:, sl * 1024:(sl + 1) * 1024]
                    cb = sl * 8
                    s1 = lncols[:, cb + 0:cb + 1]
                    s2 = lncols[:, cb + 1:cb + 2]
                    mcol = lncols[:, cb + 2:cb + 3]
                    msq = lncols[:, cb + 3:cb + 4]
                    varc = lncols[:, cb + 4:cb + 5]
                    v.wait_ge(lnx.sem, (st + 1) * 16)
                    if st >= 4:
                        v.wait_ge(rst.sem, (st - 4) + 1)
                    v.reduce_sum(s1, xs, axis=mybir.AxisListType.X)
                    v.scalar_tensor_tensor(junk[:], xs, ccol[:, 3:4], xs,
                                           ALU.mult, ALU.mult, accum_out=s2)
                    vsf.inc(v.tensor_scalar(mcol, s1, ccol[:, 1:2], None, ALU.mult))
                    v.wait_ge(vsf.sem, vsf.n)
                    v.tensor_scalar(msq, mcol, mcol, None, ALU.mult)
                    v1.inc(v.scalar_tensor_tensor(varc, s2, ccol[:, 1:2], msq,
                                                  ALU.mult, ALU.subtract))
                    if st >= 1:
                        _p1_v2(v, st - 1)
                _p1_v2(v, NT - 1)
                v.wait_ge(lnx.sem, LNX_ALL)
                for ct in range(NCT):
                    v.reduce_max(amx[:, ct:ct + 1],
                                 krf[:, ct * 1024:(ct + 1) * 1024],
                                 axis=mybir.AxisListType.X, apply_absolute_value=True)
                qnt.inc(v.reduce_max(amx[:, 8:9], amx[:, 0:8],
                                     axis=mybir.AxisListType.X))
                v.wait_ge(qnt.sem, 2)
                vsf.inc(v.reduce_max(sc1[0:1, 32:33], ring[0:1, 0:128],
                                     axis=mybir.AxisListType.X))
                v.wait_ge(vsf.sem, vsf.n)
                vsf.inc(v.reciprocal(sc1[0:1, 33:34], sc1[0:1, 32:33]))
                v.wait_ge(vsf.sem, vsf.n)
                v.tensor_scalar(sc1[0:1, 34:35], sc1[0:1, 33:34],
                                ccol[0:1, 7:8], None, ALU.mult)
                qnt.inc(v.tensor_scalar(sc1[0:1, 35:36], sc1[0:1, 32:33],
                                        ccol[0:1, 13:14], None, ALU.mult))
                v.wait_ge(qnt.sem, 35)
                for ct in range(NCT):
                    kf = krf[:, ct * 1024:(ct + 1) * 1024]
                    v.tensor_scalar(junk[:], kf, scol[:, 0:1], ccol[:, 8:9],
                                    ALU.mult, ALU.add)
                    v.tensor_scalar(junk[:], junk[:], ccol[:, 9:10], ccol[:, 7:8],
                                    ALU.add, ALU.min)
                    v.tensor_scalar(qint[:, ct * 1024:(ct + 1) * 1024], junk[:],
                                    ccol[:, 10:11], None, ALU.max)
                qnt.inc(v.nop())
                v.wait_ge(pre.sem, PRE_ALL)
                qnt.inc(v.tensor_scalar(baugb[:], baug[:], scol[0:17, 0:1], None,
                                        ALU.mult))
                v.wait_ge(aux.sem, 34)
                qnt.inc(v.tensor_scalar(hcol[:], hcol[:], ccol[:, 0:1], None,
                                        ALU.add))
                v.tensor_scalar(bqs[:], bqs[:], ccol[:, 6:7], None, ALU.mult)
                v.tensor_scalar(imat_rs[:], imat_s[:], scol[:, 0:1], None, ALU.mult)
            def p0_v2(v):
                v.wait_ge(aux.sem, 1)
                aux.inc(v.tensor_copy(strip_s[:], ring[0:16, 0:512]))
            block.vector(p0_v2)
            block.vector(p1_v)

            def p1_s(s):
                _preload_named_set(nc, s, "natural_log_exp_and_others")
                for st in range(NT):
                    sl = st % 4
                    cb = sl * 8
                    varc = lncols[:, cb + 4:cb + 5]
                    lnv = lncols[:, cb + 7:cb + 8]
                    rstd = lncols[:, cb + 6:cb + 7]
                    s.wait_ge(v1.sem, st + 1)
                    ssf.inc(s.activation(out=lnv, in_=varc, func=AF.Ln,
                                         bias=ccol[:, 2:3], scale=ccol[:, 3:4]))
                    s.wait_ge(ssf.sem, ssf.n)
                    rst.inc(s.activation(out=rstd, in_=lnv, func=AF.Exp,
                                         bias=ccol[:, 12:13], scale=ccol[:, 4:5]))
            block.scalar(p1_s)

            def p0_t(t):
                t.wait_ge(pre.sem, 32)
                aux.inc(t.matmul(ring[0:16, 0:512], relb_s[:], oh_s[:],
                                 start=True, stop=True))
                t.wait_ge(qnt.sem, 1)
                t.wait_ge(aux.sem, 2)
                qnt.inc(t.matmul(ring[0:1, 0:128], amx[:, 8:9],
                                 imat_s[:], start=True, stop=True))
            block.tensor(p0_t)

            def pg_mid(g):
                g.wait_ge(aux.sem, 2)
                aux.inc(g.dma_start(out=strip_hb.ap(), in_=strip_s[:]), 16)
                g.wait_ge(aux.sem, 18)
                aux.inc(g.dma_start(
                    out=hcol[:], in_=AP(strip_hb, 64, [[0, 128], [512, H]]),
                    allow_slow_non_contiguous=True), 16)
                for st in range(NT):
                    sl = st % 4
                    g.wait_ge(v2.sem, st + 1)
                    xnw.inc(g.dma_start(
                        out=xn_hb.ap()[st * 128:(st + 1) * 128, :],
                        in_=xnr[:, sl * 1024:(sl + 1) * 1024]), 16)
                g.wait_ge(qnt.sem, 3)
                g.dma_start(out=AP(sc_hb, 0, [[1, 1], [1, 2]]),
                            in_=sc1[0:1, 34:36]).then_inc(qnt.sem, 16)
                qnt.n += 16
                g.wait_ge(qnt.sem, 19)
                qnt.inc(g.dma_start(out=scol[:, 0:2],
                                    in_=AP(sc_hb, 0, [[0, 128], [1, 2]])), 16)
                if lvl < 4:
                    return
                for ct in range(NCT):
                    stld.inc(g.dma_start(
                        out=wbuf[:, ct * 1024:(ct + 1) * 1024],
                        in_=wq_h.ap()[ct * 128:(ct + 1) * 128, :]), 16)
                for k in range(N_FFN):
                    st, fc = k // 2, k % 2
                    gsl = k % 4
                    g.wait_ge(pev.sem, N_UT + k + 1)
                    pst.inc(g.dma_start(
                        out=g_hb.ap()[st, :, fc * 512:(fc + 1) * 512],
                        in_=g_ring[:, gsl * 512:(gsl + 1) * 512]), 16)
                if lvl < 5:
                    return
                for k in range(N_QK):
                    dt, sch = k // NQC, k % NQC
                    esl = k % 3
                    g.wait_ge(pev.sem, N_UT + N_FFN + k + 1)
                    pst.inc(g.dma_start(
                        out=qT_hb.ap()[dt, :, sch * 512:(sch + 1) * 512],
                        in_=evst[:, esl * 512:(esl + 1) * 512]), 16)
                g.wait_ge(prd.sem, N_UT + N_FFN + N_QK)
                for ct in range(NCT):
                    stld.inc(g.dma_start(
                        out=wbuf[:, ct * 1024:(ct + 1) * 1024],
                        in_=wk_h.ap()[ct * 128:(ct + 1) * 128, :]), 16)
                for k in range(N_QK):
                    dt, sch = k // NQC, k % NQC
                    esl = (N_QK + k) % 3
                    g.wait_ge(pev.sem, N_UT + N_FFN + N_QK + k + 1)
                    pst.inc(g.dma_start(
                        out=kT_hb.ap()[dt, :, sch * 512:(sch + 1) * 512],
                        in_=evst[:, esl * 512:(esl + 1) * 512]), 16)
                g.wait_ge(prd.sem, N_UT + N_FFN + 2 * N_QK)
                for ct in range(NCT):
                    stld.inc(g.dma_start(
                        out=wbuf[:, ct * 1024:(ct + 1) * 1024],
                        in_=wv_h.ap()[ct * 128:(ct + 1) * 128, :]), 16)
                for k in range(N_V):
                    st, dc = k // 2, k % 2
                    esl = (2 * N_QK + k) % 3
                    g.wait_ge(pev.sem, N_UT + N_FFN + 2 * N_QK + k + 1)
                    pst.inc(g.dma_start(
                        out=v_hb.ap()[2 * dc, st, :, :],
                        in_=evst[:, esl * 512:esl * 512 + 256]), 16)
                    pst.inc(g.dma_start(
                        out=v_hb.ap()[2 * dc + 1, st, :, :],
                        in_=evst[:, esl * 512 + 256:esl * 512 + 512]), 16)
            block.gpsimd(pg_mid)
            STLD_W = 3 * NCT * 16

            def p2_sy(sy):
                sy.wait_ge(xnw.sem, NT * 16)
                for ct in range(NCT):
                    xnt.inc(sy.dma_start_transpose(
                        xnT[:, ct * S:(ct + 1) * S],
                        xn_hb.ap()[:, ct * 128:(ct + 1) * 128]), 16)
            block.sync(p2_sy)
            XNT_ALL = NCT * 16

            def p34_t(t):
                t.wait_ge(xnt.sem, XNT_ALL)
                t.wait_ge(qnt.sem, 37)
                ch = 0
                if lvl < 4:
                    return
                for sch in range(NQC):
                    bk = (ring[:, 0:512], ring[:, 512:1024], ring[:, 1024:1536])[ch % 3]
                    if ch >= 3:
                        t.wait_ge(pev.sem, ch - 2)
                    mm = None
                    for ct in range(NCT):
                        mm = t.matmul(bk[0:16, :], a_s[:, ct * 16:(ct + 1) * 16],
                                      xnT[:, ct * S + sch * 512:ct * S + sch * 512 + 512],
                                      start=(ct == 0), stop=(ct == NCT - 1))
                    prd.inc(mm)
                    ch += 1
                for k in range(N_FFN):
                    st, fc = k // 2, k % 2
                    bk = (ring[:, 0:512], ring[:, 512:1024], ring[:, 1024:1536])[ch % 3]
                    if ch >= 3:
                        t.wait_ge(pev.sem, ch - 2)
                    for ct in range(NCT):
                        t.matmul(bk[:], xnT[:, ct * S + st * 128:ct * S + st * 128 + 128],
                                 qint[:, ct * 1024 + fc * 512:ct * 1024 + fc * 512 + 512],
                                 start=(ct == 0), stop=False)
                    mm = t.matmul(bk[:], uTa[:, st * 128:(st + 1) * 128],
                                  baugb[:, fc * 512:(fc + 1) * 512],
                                  start=False, stop=True)
                    prd.inc(mm)
                    ch += 1
                if lvl < 5:
                    return
                for wsel in range(2):
                    t.wait_ge(stld.sem, (wsel + 1) * NCT * 16)
                    for k in range(N_QK):
                        dt, sch = k // NQC, k % NQC
                        bk = (ring[:, 0:512], ring[:, 512:1024], ring[:, 1024:1536])[ch % 3]
                        t.wait_ge(pev.sem, ch - 2)
                        mm = None
                        for ct in range(NCT):
                            mm = t.matmul(
                                bk[:],
                                wbuf[:, ct * 1024 + dt * 128:ct * 1024 + dt * 128 + 128],
                                xnT[:, ct * S + sch * 512:ct * S + sch * 512 + 512],
                                start=(ct == 0), stop=(ct == NCT - 1))
                        prd.inc(mm)
                        ch += 1
                t.wait_ge(stld.sem, 3 * NCT * 16)
                for k in range(N_V):
                    st, dc = k // 2, k % 2
                    bk = (ring[:, 0:512], ring[:, 512:1024], ring[:, 1024:1536])[ch % 3]
                    t.wait_ge(pev.sem, ch - 2)
                    for ct in range(NCT):
                        t.matmul(bk[:], xnT[:, ct * S + st * 128:ct * S + st * 128 + 128],
                                 wbuf[:, ct * 1024 + dc * 512:ct * 1024 + dc * 512 + 512],
                                 start=(ct == 0), stop=False)
                    mm = t.matmul(bk[:], onesr[0:1, 0:128],
                                  bvrow[0:1, dc * 512:(dc + 1) * 512],
                                  start=False, stop=True)
                    prd.inc(mm)
                    ch += 1
            block.tensor(p34_t)

            def p34_v(v):
                ch = 0
                if lvl < 4:
                    return
                for sch in range(NQC):
                    bk = (ring[:, 0:512], ring[:, 512:1024], ring[:, 1024:1536])[ch % 3]
                    v.wait_ge(prd.sem, ch + 1)
                    pev.inc(v.tensor_copy(uTa[0:16, sch * 512:(sch + 1) * 512],
                                          bk[0:16, :]))
                    ch += 1
                for k in range(N_FFN):
                    bk = (ring[:, 0:512], ring[:, 512:1024], ring[:, 1024:1536])[ch % 3]
                    gsl = k % 4
                    v.wait_ge(rst.sem, NT + k + 1)
                    if k >= 4:
                        v.wait_ge(pst.sem, pst_after_store("g", k - 4))
                    pev.inc(v.scalar_tensor_tensor(
                        g_ring[:, gsl * 512:(gsl + 1) * 512], th_s[:],
                        ccol[:, 3:4], bk[:], ALU.add, ALU.mult))
                    ch += 1
                if lvl < 5:
                    return
                for wsel in range(2):
                    bias = bqs if wsel == 0 else bks
                    kind = "q" if wsel == 0 else "k"
                    for k in range(N_QK):
                        dt = k // NQC
                        q3 = wsel * N_QK + k
                        bk = (ring[:, 0:512], ring[:, 512:1024], ring[:, 1024:1536])[ch % 3]
                        esl = q3 % 3
                        v.wait_ge(prd.sem, ch + 1)
                        if q3 >= 3:
                            p3 = q3 - 3
                            v.wait_ge(pst.sem, pst_after_store(
                                "q" if p3 < N_QK else "k",
                                p3 if p3 < N_QK else p3 - N_QK))
                        if wsel == 0:
                            pev.inc(v.tensor_scalar(
                                evst[:, esl * 512:(esl + 1) * 512], bk[:],
                                ccol[:, 6:7], bias[:, dt:dt + 1],
                                ALU.mult, ALU.add))
                        else:
                            pev.inc(v.tensor_single_scalar(
                                evst[:, esl * 512:(esl + 1) * 512], bk[:],
                                bias[:, dt:dt + 1], ALU.add))
                        ch += 1
                for k in range(N_V):
                    q3 = 2 * N_QK + k
                    bk = (ring[:, 0:512], ring[:, 512:1024], ring[:, 1024:1536])[ch % 3]
                    esl = q3 % 3
                    v.wait_ge(prd.sem, ch + 1)
                    p3 = q3 - 3
                    v.wait_ge(pst.sem, pst_after_store(
                        "k" if p3 < 2 * N_QK else "v",
                        p3 - N_QK if p3 < 2 * N_QK else p3 - 2 * N_QK))
                    pev.inc(v.tensor_copy(evst[:, esl * 512:(esl + 1) * 512], bk[:]))
                    ch += 1
            block.vector(p34_v)

            def p4_s(s):
                if lvl < 4:
                    return
                s.preload_act_table(AF.Tanh)
                for k in range(N_FFN):
                    bk = (ring[:, 0:512], ring[:, 512:1024], ring[:, 1024:1536])[(N_UT + k) % 3]
                    s.wait_ge(prd.sem, N_UT + k + 1)
                    if k >= 1:
                        s.wait_ge(pev.sem, N_UT + k)
                    rst.inc(s.activation(out=th_s[:], in_=bk[:], func=AF.Tanh,
                                         bias=ccol[:, 12:13], scale=scol[:, 1:2]))
            block.scalar(p4_s)

        if lvl >= 6:
            es2 = contextlib.ExitStack()
            with es2:
                kTs = es2.enter_context(nc.sbuf_tensor("kTs", [128, 2 * 2 * S], BF16))
                v_s = es2.enter_context(nc.sbuf_tensor("v_s", [128, 2 * NT * 256], BF16))
                qsl = es2.enter_context(nc.sbuf_tensor("qsl", [128, 2 * 2 * 512], BF16))
                attnU = es2.enter_context(nc.sbuf_tensor("attnU", [128, 8 * 1024], BF16))
                btile = es2.enter_context(nc.sbuf_tensor("btile", [128, 2 * 8 * 128], BF16))
                sTst = es2.enter_context(nc.sbuf_tensor("sTst", [128, 4 * 512], BF16))
                dnst = es2.enter_context(nc.sbuf_tensor("dnst", [128, 2 * 512], F32))
                e_s = es2.enter_context(nc.sbuf_tensor("e_s", [128, 2 * 1024], BF16))
                e_g = es2.enter_context(nc.sbuf_tensor("e_g", [128, 2 * 1024], BF16))
                e_x = es2.enter_context(nc.sbuf_tensor("e_x", [128, 2 * 1024], F32))
                e_o = es2.enter_context(nc.sbuf_tensor("e_o", [128, 2 * 1024], F32))
                e_r = es2.enter_context(nc.sbuf_tensor("e_r", [128, 2 * 16], F32))
                denT = es2.enter_context(nc.sbuf_tensor("denT", [128, 512], F32))
                NW = 4
                W_PER_QC = 16 * NW

                def w_index(hq, qc, kp, hi):
                    return ((hq * NQC + qc) * 16 + kp) * NW + hi

                STLD_HQ = (2 + NT) * 16

                unit_info = []
                win_units = {}
                den_unit = {}
                dtx_unit = {}
                for hq_ in range(4):
                    for qc_ in range(NQC):
                        gqc_ = hq_ * NQC + qc_
                        for kp_ in range(16):
                            for hi_ in range(4):
                                w_ = w_index(hq_, qc_, kp_, hi_)
                                u0 = len(unit_info)
                                unit_info.append(("sc", w_))
                                unit_info.append(("sc", w_))
                                win_units[w_] = (u0, u0 + 1)
                            if kp_ >= 1:
                                den_unit[(gqc_, kp_ - 1)] = len(unit_info)
                                unit_info.append(("den", gqc_ * 16 + kp_ - 1))
                        den_unit[(gqc_, 15)] = len(unit_info)
                        unit_info.append(("den", gqc_ * 16 + 15))
                        dtx_unit[gqc_] = len(unit_info)
                        unit_info.append(("dtx", gqc_))

                def unit_free_wait(eng, u):
                    if u < 6:
                        return
                    kind, p = unit_info[u - 6]
                    if kind == "sc":
                        eng.wait_ge(ex.sem, p + 1)
                    elif kind == "den":
                        gq, kp_ = p // 16, p % 16
                        eng.wait_ge(dvd.sem, gq * 17 + kp_ + 1)
                    else:
                        eng.wait_ge(dvd.sem, (p + 1) * 17)

                def ubank(u):
                    return (u % 6) * 512

                def p6_sy(sy):
                    sy.wait_ge(pst.sem, PST_ALL)
                    for hq in range(4):
                        sset = hq % 2
                        if hq >= 2:
                            sy.wait_ge(cns.sem, (hq - 1) * NQC * 16)
                        for dt2 in range(2):
                            stld.inc(sy.dma_start(
                                out=kTs[:, (sset * 2 + dt2) * S:(sset * 2 + dt2 + 1) * S],
                                in_=kT_hb.ap()[2 * hq + dt2, :, :]), 16)
                        for kt in range(NT):
                            stld.inc(sy.dma_start(
                                out=v_s[:, sset * NT * 256 + kt * 256:
                                        sset * NT * 256 + (kt + 1) * 256],
                                in_=v_hb.ap()[hq, kt, :, :]), 16)
                        for qc in range(NQC):
                            gqc = hq * NQC + qc
                            qset = gqc % 2
                            if gqc >= 2:
                                sy.wait_ge(sc.sem, (gqc - 1) * W_PER_QC)
                            for dt2 in range(2):
                                qld.inc(sy.dma_start(
                                    out=qsl[:, (qset * 2 + dt2) * 512:
                                            (qset * 2 + dt2 + 1) * 512],
                                    in_=qT_hb.ap()[2 * hq + dt2, :,
                                                   qc * 512:(qc + 1) * 512]), 16)
                block.sync(p6_sy)

                def p6_g(g):
                    g.wait_ge(aux.sem, 18)
                    for hq in range(4):
                        bset = hq % 2
                        if hq >= 2:
                            g.wait_ge(cns.sem, (hq - 1) * NQC * 16)
                        for hi in range(4):
                            h = 4 * hq + hi
                            for dt in range(2):
                                j0 = 128 * dt + 1
                                btl.inc(g.dma_start(
                                    out=btile[:, (bset * 8 + hi * 2 + dt) * 128:
                                              (bset * 8 + hi * 2 + dt + 1) * 128],
                                    in_=AP(strip_hb, h * 512 + j0, [[1, 128], [1, 128]])), 16)
                        for qc in range(NQC):
                            gqc = hq * NQC + qc
                            esl = gqc % 2
                            hp = 2 * hq
                            g.wait_ge(epi.sem, gqc * 2 + 1)
                            est.inc(g.dma_start(
                                out=sT_hb.ap()[hp * 128:(hp + 1) * 128,
                                               qc * 512:(qc + 1) * 512],
                                in_=sTst[:, esl * 1024:esl * 1024 + 512]), 16)
                            g.wait_ge(epi.sem, gqc * 2 + 2)
                            est.inc(g.dma_start(
                                out=sT_hb.ap()[(hp + 1) * 128:(hp + 2) * 128,
                                               qc * 512:(qc + 1) * 512],
                                in_=sTst[:, esl * 1024 + 512:esl * 1024 + 1024]), 16)
                block.gpsimd(p6_g)

                def _p6_cons(t, hq, qc, kp, sset):
                    gqc = hq * NQC + qc
                    kpg = gqc * 16 + kp
                    t.wait_ge(ex.sem, kpg * NW + NW)
                    if kp == 0 and gqc >= 1:
                        t.wait_ge(epi.sem, gqc * 2)
                    first = (kp == 0)
                    lastg = (kp == 15)
                    lastctx = None
                    for j in range(2):
                        kt = 2 * kp + j
                        for pi, ctx in ((0, ctxA), (1, ctxB)):
                            for ci in range(2):
                                hi = pi * 2 + ci
                                w = w_index(hq, qc, kp, hi)
                                au = attnU[:, (w % 8) * 1024 + j * 512:
                                           (w % 8) * 1024 + j * 512 + 512]
                                lastctx = t.matmul(
                                         ctx[ci * 64:(ci + 1) * 64, :],
                                         v_s[:, sset * NT * 256 + kt * 256 + hi * 64:
                                             sset * NT * 256 + kt * 256 + hi * 64 + 64],
                                         au,
                                         start=(first and j == 0),
                                         stop=(lastg and j == 1),
                                         tile_position=(0, ci * 64))
                    cns.inc(lastctx)
                    du = den_unit[(gqc, kp)]
                    unit_free_wait(t, du)
                    boff = ubank(du)
                    last = None
                    for j in range(2):
                        for hi in range(4):
                            w = w_index(hq, qc, kp, hi)
                            au = attnU[:, (w % 8) * 1024 + j * 512:
                                       (w % 8) * 1024 + j * 512 + 512]
                            last = t.matmul(ring[32 * hi:32 * (hi + 1),
                                                 boff:boff + 512], twos_s[:], au,
                                            start=(j == 0), stop=(j == 1),
                                            tile_position=(0, 32 * hi))
                    dnu.inc(last)

                def p6_t(t):
                    t.wait_ge(pev.sem, N_CH)
                    for hq in range(4):
                        sset = hq % 2
                        bset = hq % 2
                        t.wait_ge(stld.sem, STLD_W + (hq + 1) * STLD_HQ)
                        t.wait_ge(btl.sem, (hq + 1) * 8 * 16)
                        for qc in range(NQC):
                            gqc = hq * NQC + qc
                            qset = gqc % 2
                            t.wait_ge(qld.sem, (gqc + 1) * 2 * 16)
                            for kp in range(16):
                                def _score_mm(t, hi, j, stop):
                                    kt = 2 * kp + j
                                    w = w_index(hq, qc, kp, hi)
                                    boff = ubank(win_units[w][j])
                                    dtl = sset * 2 + hi // 2
                                    rb = (hi % 2) * 64
                                    return t.matmul(
                                        ring[:, boff:boff + 512],
                                        kTs[rb:rb + 64, dtl * S + kt * 128:
                                            dtl * S + kt * 128 + 128],
                                        qsl[rb:rb + 64,
                                            (qset * 2 + hi // 2) * 512:
                                            (qset * 2 + hi // 2 + 1) * 512],
                                        start=True, stop=stop,
                                        tile_position=((hi % 2) * 64, 0))

                                any_diag = any(_kt_class(2 * kp + j, qc) == "diag"
                                               for j in range(2))
                                if not any_diag:
                                    for ha in (0, 2):
                                        hb = ha + 1
                                        for hx in (ha, hb):
                                            for j in range(2):
                                                unit_free_wait(
                                                    t, win_units[w_index(
                                                        hq, qc, kp, hx)][j])
                                        for j in range(2):
                                            _score_mm(t, ha, j, True)
                                            last = _score_mm(t, hb, j, True)
                                        sc.inc(last, 2)
                                else:
                                    for hi in range(4):
                                        w = w_index(hq, qc, kp, hi)
                                        uj = win_units[w]
                                        last = None
                                        for j in range(2):
                                            kt = 2 * kp + j
                                            unit_free_wait(t, uj[j])
                                            boff = ubank(uj[j])
                                            cls = _kt_class(kt, qc)
                                            last = _score_mm(t, hi, j,
                                                             cls != "diag")
                                            if cls == "diag":
                                                subs = [qt - 4 * qc
                                                        for qt in (kt, kt + 1)
                                                        if 4 * qc <= qt <= 4 * qc + 3]
                                                for idx, sub in enumerate(subs):
                                                    dt = (sub + 4 * qc) - kt
                                                    last = t.matmul(
                                                        ring[:, boff + sub * 128:
                                                             boff + sub * 128 + 128],
                                                        jmat_s[:],
                                                        btile[:, (bset * 8 + hi * 2 + dt) * 128:
                                                              (bset * 8 + hi * 2 + dt + 1) * 128],
                                                        start=False,
                                                        stop=(idx == len(subs) - 1))
                                        sc.inc(last)
                                if kp >= 1:
                                    _p6_cons(t, hq, qc, kp - 1, sset)
                            _p6_cons(t, hq, qc, 15, sset)
                            t.wait_ge(dvd.sem, gqc * 17 + 16)
                            tu = dtx_unit[gqc]
                            unit_free_wait(t, tu)
                            tb = ubank(tu)
                            mm = None
                            for i in range(4):
                                mm = t.matmul(
                                    ring[:, tb + i * 128:tb + (i + 1) * 128],
                                    dnst[:, (gqc % 2) * 512 + i * 128:
                                         (gqc % 2) * 512 + (i + 1) * 128],
                                    imat_rs[:], start=True, stop=True)
                            dnu.inc(mm)
                block.tensor(p6_t)

                def p6_s(s):
                    _preload_named_set(nc, s, "exp_and_others")
                    for hq in range(4):
                        for qc in range(NQC):
                            for kp in range(16):
                                kpg = (hq * NQC + qc) * 16 + kp
                                if kpg >= 2:
                                    pk = kpg - 2
                                    s.wait_ge(dnu.sem,
                                              (pk // 16) * 17 + (pk % 16) + 1)
                                for hi in range(4):
                                    h = 4 * hq + hi
                                    w = w_index(hq, qc, kp, hi)
                                    uj = win_units[w]
                                    b0, b1 = ubank(uj[0]), ubank(uj[1])
                                    s.wait_ge(sc.sem, w + 1)
                                    cls = [_kt_class(2 * kp + j, qc) for j in range(2)]
                                    def bias_of(c):
                                        return (hcol[:, h:h + 1] if c == "fut"
                                                else ccol[:, 0:1])
                                    dsl = (w % 8) * 1024
                                    same_bias = (cls[0] == "fut") == (cls[1] == "fut")
                                    contig = (b1 == b0 + 512)
                                    if same_bias and contig:
                                        last = s.activation(
                                            out=attnU[:, dsl:dsl + 1024],
                                            in_=ring[:, b0:b0 + 1024],
                                            func=AF.Exp, bias=bias_of(cls[0]))
                                    else:
                                        s.activation(
                                            out=attnU[:, dsl:dsl + 512],
                                            in_=ring[:, b0:b0 + 512],
                                            func=AF.Exp, bias=bias_of(cls[0]))
                                        last = s.activation(
                                            out=attnU[:, dsl + 512:dsl + 1024],
                                            in_=ring[:, b1:b1 + 512],
                                            func=AF.Exp, bias=bias_of(cls[1]))
                                    ex.inc(last)
                block.scalar(p6_s)

                def p6_v(v):
                    for hq in range(4):
                        for qc in range(NQC):
                            gqc = hq * NQC + qc
                            esl = gqc % 2
                            for kp in range(16):
                                du = den_unit[(gqc, kp)]
                                boff = ubank(du)
                                v.wait_ge(dnu.sem, gqc * 17 + kp + 1)
                                dst = dnst[:, esl * 512:(esl + 1) * 512]
                                if kp == 0:
                                    dvd.inc(v.tensor_copy(dst, ring[:, boff:boff + 512]))
                                else:
                                    dvd.inc(v.tensor_tensor(dst, dst,
                                                            ring[:, boff:boff + 512],
                                                            ALU.add))
                            v.wait_ge(cns.sem, (gqc + 1) * 16)
                            if gqc >= 2:
                                v.wait_ge(est.sem, (gqc - 1) * 2 * 16)
                            epi.inc(v.tensor_copy(sTst[:, esl * 1024:esl * 1024 + 512],
                                                  ctxA[:]))
                            epi.inc(v.tensor_copy(sTst[:, esl * 1024 + 512:
                                                        esl * 1024 + 1024], ctxB[:]))
                            v.wait_ge(dnu.sem, (gqc + 1) * 17)
                            tb = ubank(dtx_unit[gqc])
                            dvd.inc(v.tensor_copy(
                                AP(denT, qc * 64 + 4 * hq,
                                   [[512, 128], [16, 4], [1, 4]]),
                                AP(ring, tb, [[3072, 128], [128, 4], [32, 4]])))
                    for qt in range(NT):
                        esl = qt % 2
                        v.wait_ge(eld.sem, (qt + 1) * 3 * 16)
                        if qt >= 2:
                            v.wait_ge(ost.sem, (qt - 1) * 16)
                        vsf.inc(v.reciprocal(e_r[:, esl * 16:(esl + 1) * 16],
                                             denT[:, qt * 16:(qt + 1) * 16]))
                        v.wait_ge(vsf.sem, vsf.n)
                        rE = AP(e_r, esl * 16, [[32, 128], [1, 16], [0, 64]])
                        if dump == "edn":
                            ecp.inc(v.tensor_copy(
                                e_o[:, esl * 1024:esl * 1024 + 512], denT[:]))
                            continue
                        if dump == "es":
                            ecp.inc(v.tensor_copy(
                                e_o[:, esl * 1024:(esl + 1) * 1024],
                                e_s[:, esl * 1024:(esl + 1) * 1024]))
                            continue
                        if dump == "eg":
                            ecp.inc(v.tensor_copy(
                                e_o[:, esl * 1024:(esl + 1) * 1024],
                                e_g[:, esl * 1024:(esl + 1) * 1024]))
                            continue
                        if dump == "er":
                            ecp.inc(v.tensor_copy(
                                e_o[:, esl * 1024:(esl + 1) * 1024], rE))
                            continue
                        v.tensor_tensor(e_o[:, esl * 1024:(esl + 1) * 1024],
                                        e_s[:, esl * 1024:(esl + 1) * 1024], rE,
                                        ALU.mult)
                        v.tensor_tensor(e_o[:, esl * 1024:(esl + 1) * 1024],
                                        e_o[:, esl * 1024:(esl + 1) * 1024],
                                        e_g[:, esl * 1024:(esl + 1) * 1024], ALU.mult)
                        ecp.inc(v.tensor_tensor(e_o[:, esl * 1024:(esl + 1) * 1024],
                                                e_o[:, esl * 1024:(esl + 1) * 1024],
                                                e_x[:, esl * 1024:(esl + 1) * 1024],
                                                ALU.add))
                block.vector(p6_v)

                def p7_sy(sy):
                    for qt in range(NT):
                        esl = qt % 2
                        qcq = qt // 4
                        sy.wait_ge(est.sem, (3 * NQC + qcq + 1) * 2 * 16)
                        if qt >= 2:
                            sy.wait_ge(ecp.sem, qt - 1)
                        eld.inc(sy.dma_start_transpose(
                            e_s[:, esl * 1024:(esl + 1) * 1024],
                            sT_hb.ap()[:, qt * 128:(qt + 1) * 128]), 16)
                        eld.inc(sy.dma_start(out=e_g[:, esl * 1024:(esl + 1) * 1024],
                                             in_=g_hb.ap()[qt, :, :]), 16)
                        eld.inc(sy.dma_start(out=e_x[:, esl * 1024:(esl + 1) * 1024],
                                             in_=x_h.ap()[qt * 128:(qt + 1) * 128, :]), 16)
                    sy.wait_ge(ost.sem, NT * 16)
                block.sync(p7_sy)

                def p7_g(g):
                    for qt in range(NT):
                        esl = qt % 2
                        g.wait_ge(ecp.sem, qt + 1)
                        ost.inc(g.dma_start(out=out_h.ap()[qt * 128:(qt + 1) * 128, :],
                                            in_=e_o[:, esl * 1024:(esl + 1) * 1024]), 16)
                block.gpsimd(p7_g)

        if lvl < 7:
            _sdmp = es.enter_context(nc.semaphore("s_dump"))
            dmp = Ctr(_sdmp)

            def pdump_g(g):
                g.wait_ge(qnt.sem, 38)
                dmp.inc(g.dma_start(out=dbg_h.ap()[:, 0:32], in_=lncols[:]), 16)
                dmp.inc(g.dma_start(out=dbg_h.ap()[:, 32:36], in_=scol[:]), 16)
                dmp.inc(g.dma_start(out=dbg_h.ap()[0:1, 36:52],
                                    in_=sc1[0:1, 28:44]), 16)
                if dump == "xn":
                    g.wait_ge(xnw.sem, NT * 16)
                    g.wait_ge(xnt.sem, XNT_ALL)
                    for st in range(NT):
                        dmp.inc(g.dma_start(
                            out=out_h.ap()[st * 128:(st + 1) * 128, :],
                            in_=xn_hb.ap()[st * 128:(st + 1) * 128, :]), 16)
                elif dump == "qint":
                    g.wait_ge(qnt.sem, 36)
                    for ct in range(NCT):
                        dmp.inc(g.dma_start(
                            out=out_h.ap()[ct * 128:(ct + 1) * 128, :],
                            in_=qint[:, ct * 1024:(ct + 1) * 1024]), 16)
                elif dump == "g":
                    g.wait_ge(pst.sem, N_FFN * 16)
                    dmp.inc(g.dma_start(
                        out=AP(out_h, 0, [[Dm, NT * 128], [1, Dm]]),
                        in_=g_hb.ap().rearrange("a p b -> (a p) b")), 16)
                elif dump == "qT":
                    g.wait_ge(pst.sem, (N_FFN + N_QK) * 16)
                    dmp.inc(g.dma_start(
                        out=AP(out_h, 0, [[S, NCT * 128], [1, S]]),
                        in_=qT_hb.ap().rearrange("a p b -> (a p) b")), 16)
                elif dump == "kT":
                    g.wait_ge(pst.sem, (N_FFN + 2 * N_QK) * 16)
                    dmp.inc(g.dma_start(
                        out=AP(out_h, 0, [[S, NCT * 128], [1, S]]),
                        in_=kT_hb.ap().rearrange("a p b -> (a p) b")), 16)
                elif dump == "v":
                    g.wait_ge(pst.sem, PST_ALL)
                    dmp.inc(g.dma_start(
                        out=AP(out_h, 0, [[256, 4 * NT * 128], [1, 256]]),
                        in_=v_hb.ap().rearrange("a b p c -> (a b p) c")), 16)
                elif dump == "sT":
                    g.wait_ge(est.sem, est.n)
                    dmp.inc(g.dma_start(
                        out=AP(out_h, 0, [[S, Dm], [1, S]]),
                        in_=sT_hb.ap()), 16)
                elif dump == "den":
                    g.wait_ge(est.sem, est.n)
                    dmp.inc(g.dma_start(
                        out=AP(out_h, 0, [[S, H], [1, S]]),
                        in_=den_hb.ap()), 16)
            block.gpsimd(pdump_g)

            def pdump_sy(sy):
                if dmp.n:
                    sy.wait_ge(dmp.sem, dmp.n)
                if dump == "w0":
                    sy.wait_ge(dbgw.sem, 33)
            block.sync(pdump_sy)

    return nc


_NC_CACHE = None


def kernel(**inputs):
    global _NC_CACHE
    oh, jmat, imat = _host_consts()
    x = np.ascontiguousarray(inputs["L0_attn_hidden_states"], dtype=np.float32)
    base = {
        "wq": np.ascontiguousarray(inputs["L0_attn_Wq"], dtype=np.float32),
        "bq": np.ascontiguousarray(inputs["L0_attn_bq"], dtype=np.float32),
        "wk": np.ascontiguousarray(inputs["L0_attn_Wk"], dtype=np.float32),
        "bk": np.ascontiguousarray(inputs["L0_attn_bk"], dtype=np.float32),
        "wv": np.ascontiguousarray(inputs["L0_attn_Wv"], dtype=np.float32),
        "bv": np.ascontiguousarray(inputs["L0_attn_bv"], dtype=np.float32),
        "relb": np.ascontiguousarray(inputs["L0_attn_rel_bias"], dtype=np.float32),
        "ker": np.ascontiguousarray(inputs["L0_ffn_kernel"], dtype=np.float32),
        "fbias": np.ascontiguousarray(inputs["L0_ffn_bias"], dtype=np.float32),
        "a_w": np.ascontiguousarray(inputs["L0_ffn_a"], dtype=np.float32),
        "b_w": np.ascontiguousarray(inputs["L0_ffn_b"], dtype=np.float32),
        "onehot": oh, "jmat": jmat, "imat": imat,
    }
    import os
    lvl = int(os.environ.get("KLVL", "7"))
    dump = os.environ.get("KDUMP") or None
    if _NC_CACHE is None:
        _NC_CACHE = build_nc(lvl, dump)
    nc = _NC_CACHE
    in_maps = [dict(base, x=np.ascontiguousarray(x[b])) for b in range(8)]
    res = run_bass_kernel_spmd(nc, in_maps, core_ids=list(range(8)))
    return np.stack([res.results[b]["out"] for b in range(8)], axis=0)


# version 18
# speedup vs baseline: 1.2819x; 1.0227x over previous
import math
import numpy as np
import ml_dtypes
import concourse.bass as bass
import concourse.mybir as mybir
from concourse.bass import AP
from concourse.bass_utils import run_bass_kernel_spmd

F32 = mybir.dt.float32
BF16 = mybir.dt.bfloat16
FP16 = mybir.dt.float16
AF = mybir.ActivationFunctionType
ALU = mybir.AluOpType

S = 4096
Dm = 1024
H = 16
NT = 32
NCT = 8
NQC = 8
CEXP = 8.0


def _bucket(delta):
    n = max(delta, 0)
    if n == 0:
        return 0
    if n < 16:
        return n
    return min(16 + int(math.log(n / 16.0) / math.log(2.0) * 16.0), 31)


def _host_consts():
    oh = np.zeros((32, 512), dtype=np.float32)
    for j in range(512):
        oh[_bucket(j - 128), j] += 1.0
        oh[31, j] -= 1.0
    jmat = np.zeros((128, 128), dtype=np.float32)
    for i in range(128):
        jmat[i, 127 - i] = 1.0
    imat = np.eye(128, dtype=np.float32)
    return oh, jmat, imat


class Ctr:
    def __init__(self, sem):
        self.sem = sem
        self.n = 0

    def inc(self, inst, k=1):
        inst.then_inc(self.sem, k)
        self.n += k
        return self.n


def _kt_class(kt, qc):
    qlo, qhi = 4 * qc, 4 * qc + 3
    if kt > qhi:
        return "fut"
    if kt >= qlo - 1:
        return "diag"
    return "past"


def _preload_named_set(nc, eng, name):
    from concourse.hw_specs import get_activation_tables
    for idx, nm in enumerate(get_activation_tables(nc.m.arch)):
        if nm == name:
            return eng.add_instruction(mybir.InstLoadActFuncSet(
                name=nc.get_next_instruction_name(), act_func_set_id=idx,
                ins=[], outs=[]))
    raise ValueError(name)


def build_nc(lvl=7, dump=None):
    nc = bass.Bass("TRN2", target_bir_lowering=False)

    x_h = nc.dram_tensor("x", [S, Dm], F32, kind="ExternalInput")
    wq_h = nc.dram_tensor("wq", [Dm, Dm], F32, kind="ExternalInput")
    bq_h = nc.dram_tensor("bq", [Dm], F32, kind="ExternalInput")
    wk_h = nc.dram_tensor("wk", [Dm, Dm], F32, kind="ExternalInput")
    bk_h = nc.dram_tensor("bk", [Dm], F32, kind="ExternalInput")
    wv_h = nc.dram_tensor("wv", [Dm, Dm], F32, kind="ExternalInput")
    bv_h = nc.dram_tensor("bv", [Dm], F32, kind="ExternalInput")
    rb_h = nc.dram_tensor("relb", [32, H], F32, kind="ExternalInput")
    kr_h = nc.dram_tensor("ker", [Dm, Dm], F32, kind="ExternalInput")
    fb_h = nc.dram_tensor("fbias", [Dm], F32, kind="ExternalInput")
    a_h = nc.dram_tensor("a_w", [Dm, 16], F32, kind="ExternalInput")
    b_h = nc.dram_tensor("b_w", [16, Dm], F32, kind="ExternalInput")
    oh_h = nc.dram_tensor("onehot", [32, 512], F32, kind="ExternalInput")
    jm_h = nc.dram_tensor("jmat", [128, 128], F32, kind="ExternalInput")
    im_h = nc.dram_tensor("imat", [128, 128], F32, kind="ExternalInput")
    out_h = nc.dram_tensor("out", [S, Dm], F32, kind="ExternalOutput")
    dbg_h = nc.dram_tensor("dbg", [128, 64], F32, kind="ExternalOutput")
    dbg2_h = nc.dram_tensor("dbg2", [128, 4096], F32, kind="ExternalOutput")

    xn_hb = nc.dram_tensor("xn_hb", [S, Dm], BF16, kind="Internal")
    strip_hb = nc.dram_tensor("strip_hb", [H, 512], F32, kind="Internal")
    qT_hb = nc.dram_tensor("qT_hb", [NCT, 128, S], BF16, kind="Internal")
    kT_hb = nc.dram_tensor("kT_hb", [NCT, 128, S], BF16, kind="Internal")
    v_hb = nc.dram_tensor("v_hb", [4, NT, 128, 256], BF16, kind="Internal")
    g_hb = nc.dram_tensor("g_hb", [NT, 128, Dm], BF16, kind="Internal")
    sT_hb = nc.dram_tensor("sT_hb", [Dm, S], BF16, kind="Internal")
    den_hb = nc.dram_tensor("den_hb", [H, S], FP16, kind="Internal")
    sc_hb = nc.dram_tensor("sc_hb", [4], F32, kind="Internal")

    N_UT, N_FFN, N_QK, N_V = NQC, NT * 2, NCT * NQC, NT * 2
    N_CH = N_UT + N_FFN + 2 * N_QK + N_V

    def pst_after_store(idx_kind, k):
        if idx_kind == "g":
            return (k + 1) * 16
        if idx_kind == "q":
            return (N_FFN + k + 1) * 16
        if idx_kind == "k":
            return (N_FFN + N_QK + k + 1) * 16
        return (N_FFN + 2 * N_QK + 2 * (k + 1)) * 16

    PST_ALL = (N_FFN + 2 * N_QK + 2 * N_V) * 16

    import contextlib
    es = contextlib.ExitStack()
    with es:
        ring = es.enter_context(nc.psum_tensor("ring", [128, 3072], F32))
        ctxA = es.enter_context(nc.psum_tensor("ctxA", [128, 512], F32))
        ctxB = es.enter_context(nc.psum_tensor("ctxB", [128, 512], F32))
        ccol = es.enter_context(nc.sbuf_tensor("ccol", [128, 16], F32))
        hcol = es.enter_context(nc.sbuf_tensor("hcol", [128, H], F32))
        bqs = es.enter_context(nc.sbuf_tensor("bqs", [128, NCT], F32))
        bks = es.enter_context(nc.sbuf_tensor("bks", [128, NCT], F32))
        bvrow = es.enter_context(nc.sbuf_tensor("bvrow", [1, Dm], BF16))
        relb_s = es.enter_context(nc.sbuf_tensor("relb_s", [32, H], F32))
        oh_s = es.enter_context(nc.sbuf_tensor("oh_s", [32, 512], F32))
        strip_s = es.enter_context(nc.sbuf_tensor("strip_s", [16, 512], F32))
        jmat_s = es.enter_context(nc.sbuf_tensor("jmat_s", [128, 128], BF16))
        imat_s = es.enter_context(nc.sbuf_tensor("imat_s", [128, 128], F32))
        imat_rs = es.enter_context(nc.sbuf_tensor("imat_rs", [128, 128], F32))
        twos_s = es.enter_context(nc.sbuf_tensor("twos_s", [128, 32], BF16))
        onesr = es.enter_context(nc.sbuf_tensor("onesr", [1, 512], BF16))
        scol = es.enter_context(nc.sbuf_tensor("scol", [128, 4], F32))
        sc1 = es.enter_context(nc.sbuf_tensor("sc1", [1, 64], F32))
        sems = [es.enter_context(nc.semaphore("s_%d" % i)) for i in range(29)]
        (_s0, _s1, _s2, _s3, _s4, _s5, _s6, _s7, _s8, _s9, _s10, _s11, _s12,
         _s13, _s14, _s15, _s16, _s17, _s18, _s19, _s20, _s21, _s22,
         _s23, _s24, _s25, _s26, _s27, _s28) = sems
        block = es.enter_context(nc.Block())
        pre, lnx, v1, rst, v2 = Ctr(_s0), Ctr(_s1), Ctr(_s2), Ctr(_s3), Ctr(_s4)
        xnw, xnt, qnt, prd, pev = Ctr(_s5), Ctr(_s6), Ctr(_s7), Ctr(_s8), Ctr(_s9)
        pst, stld, qld, sc, ex = Ctr(_s10), Ctr(_s11), Ctr(_s12), Ctr(_s13), Ctr(_s14)
        cns, epi, est, eld, ecp = Ctr(_s15), Ctr(_s16), Ctr(_s17), Ctr(_s18), Ctr(_s19)
        ost, aux, btl = Ctr(_s20), Ctr(_s21), Ctr(_s22)
        vsf, ssf = Ctr(_s23), Ctr(_s24)
        dbgw = Ctr(_s25)
        dtx = Ctr(_s26)
        dnu, dvd = Ctr(_s27), Ctr(_s28)


        es1 = contextlib.ExitStack()
        with es1:
            xnT = es1.enter_context(nc.sbuf_tensor("xnT", [128, NCT * S], BF16))
            qint = es1.enter_context(nc.sbuf_tensor("qint", [128, NCT * 1024], BF16))
            wbuf = es1.enter_context(nc.sbuf_tensor("wbuf", [128, NCT * 1024], BF16))
            krf = es1.enter_context(nc.sbuf_tensor("krf", [128, NCT * 1024], F32))
            xr = es1.enter_context(nc.sbuf_tensor("xr", [128, 4 * 1024], F32))
            xnr = es1.enter_context(nc.sbuf_tensor("xnr", [128, 4 * 1024], BF16))
            junk = es1.enter_context(nc.sbuf_tensor("junk", [128, 1024], F32))
            lncols = es1.enter_context(nc.sbuf_tensor("lncols", [128, 4 * 8], F32))
            amx = es1.enter_context(nc.sbuf_tensor("amx", [128, 10], F32))
            evst = es1.enter_context(nc.sbuf_tensor("evst", [128, 3 * 512], BF16))
            baug = es1.enter_context(nc.sbuf_tensor("baug", [17, 1024], F32))
            baugb = es1.enter_context(nc.sbuf_tensor("baugb", [17, 1024], BF16))
            uTa = es1.enter_context(nc.sbuf_tensor("uTa", [17, S], BF16))
            a_s = es1.enter_context(nc.sbuf_tensor("a_s", [128, NCT * 16], BF16))
            th_s = es1.enter_context(nc.sbuf_tensor("th_s", [128, 512], F32))
            g_ring = es1.enter_context(nc.sbuf_tensor("g_ring", [128, 4 * 512], BF16))
            def p0_g(g):
                pre.inc(g.dma_start(out=relb_s[:], in_=rb_h.ap()), 16)
                pre.inc(g.dma_start(out=oh_s[:], in_=oh_h.ap()), 16)
                pre.inc(g.dma_start(out=jmat_s[:], in_=jm_h.ap()), 16)
                pre.inc(g.dma_start(out=imat_s[:], in_=im_h.ap()), 16)
                pre.inc(g.dma_start(out=bqs[:],
                                    in_=AP(bq_h, 0, [[1, 128], [128, NCT]]),
                                    allow_slow_non_contiguous=True), 16)
                pre.inc(g.dma_start(out=bks[:],
                                    in_=AP(bk_h, 0, [[1, 128], [128, NCT]]),
                                    allow_slow_non_contiguous=True), 16)
                pre.inc(g.dma_start(out=bvrow[:],
                                    in_=AP(bv_h, 0, [[0, 1], [1, Dm]])), 16)
                pre.inc(g.dma_start(
                    out=a_s[:], in_=AP(a_h, 0, [[16, 128], [2048, NCT], [1, 16]])), 16)
                pre.inc(g.dma_start(out=baug[0:16, :], in_=b_h.ap()), 16)
                pre.inc(g.dma_start(out=baug[16:17, :],
                                    in_=AP(fb_h, 0, [[0, 1], [1, Dm]])), 16)
                g.wait_ge(pre.sem, 161)
                pre.inc(g.dma_start(
                    out=uTa[16:17, :],
                    in_=AP(onesr, 0, [[512, 1], [0, 8], [1, 512]])), 16)
            block.gpsimd(p0_g)
            PRE_LOADS = 160

            def p0_v(v):
                v.memset(twos_s[:], 2.0)
                v.memset(onesr[:], 1.0)
                v.memset(ccol[:, 0:1], -CEXP)
                v.memset(ccol[:, 1:2], 1.0 / Dm)
                v.memset(ccol[:, 2:3], 1e-5)
                v.memset(ccol[:, 3:4], 1.0)
                v.memset(ccol[:, 4:5], -0.5)
                v.memset(ccol[:, 5:6], -1.0)
                v.memset(ccol[:, 6:7], 0.125)
                v.memset(ccol[:, 7:8], 7.0)
                v.memset(ccol[:, 8:9], 12582912.0)
                v.memset(ccol[:, 9:10], -12582912.0)
                v.memset(ccol[:, 10:11], -8.0)
                v.memset(ccol[:, 11:12], 0.5)
                v.memset(ccol[:, 12:13], 0.0)
                v.memset(ccol[:, 13:14], 1.0 / 14.0)
                pre.inc(v.memset(ccol[:, 14:15], 0.0))
            block.vector(p0_v)
            PRE_ALL = PRE_LOADS + 1 + 16

            def p1_sy(sy):
                for st in range(NT):
                    if st >= 4:
                        sy.wait_ge(v2.sem, (st - 4) + 1)
                    lnx.inc(sy.dma_start(
                        out=xr[:, (st % 4) * 1024:(st % 4 + 1) * 1024],
                        in_=x_h.ap()[st * 128:(st + 1) * 128, :]), 16)
                for ct in range(NCT):
                    lnx.inc(sy.dma_start(
                        out=krf[:, ct * 1024:(ct + 1) * 1024],
                        in_=kr_h.ap()[ct * 128:(ct + 1) * 128, :]), 16)
            block.sync(p1_sy)
            LNX_ALL = (NT + NCT) * 16

            def _p1_v2(v, st):
                sl = st % 4
                cb = sl * 8
                xs = xr[:, sl * 1024:(sl + 1) * 1024]
                mcol = lncols[:, cb + 2:cb + 3]
                rstd = lncols[:, cb + 6:cb + 7]
                mb = lncols[:, cb + 5:cb + 6]
                v.wait_ge(rst.sem, st + 1)
                if st >= 4:
                    v.wait_ge(xnw.sem, (st - 4 + 1) * 16)
                vsf.inc(v.tensor_scalar(mb, mcol, rstd, ccol[:, 5:6],
                                        ALU.mult, ALU.mult))
                v.wait_ge(vsf.sem, vsf.n)
                v2.inc(v.tensor_scalar(xnr[:, sl * 1024:(sl + 1) * 1024], xs,
                                       rstd, mb, ALU.mult, ALU.add))

            def p1_v(v):
                for st in range(NT):
                    sl = st % 4
                    xs = xr[:, sl * 1024:(sl + 1) * 1024]
                    cb = sl * 8
                    s1 = lncols[:, cb + 0:cb + 1]
                    s2 = lncols[:, cb + 1:cb + 2]
                    mcol = lncols[:, cb + 2:cb + 3]
                    msq = lncols[:, cb + 3:cb + 4]
                    varc = lncols[:, cb + 4:cb + 5]
                    v.wait_ge(lnx.sem, (st + 1) * 16)
                    if st >= 4:
                        v.wait_ge(rst.sem, (st - 4) + 1)
                    v.reduce_sum(s1, xs, axis=mybir.AxisListType.X)
                    v.scalar_tensor_tensor(junk[:], xs, ccol[:, 3:4], xs,
                                           ALU.mult, ALU.mult, accum_out=s2)
                    vsf.inc(v.tensor_scalar(mcol, s1, ccol[:, 1:2], None, ALU.mult))
                    v.wait_ge(vsf.sem, vsf.n)
                    v.tensor_scalar(msq, mcol, mcol, None, ALU.mult)
                    v1.inc(v.scalar_tensor_tensor(varc, s2, ccol[:, 1:2], msq,
                                                  ALU.mult, ALU.subtract))
                    if st >= 1:
                        _p1_v2(v, st - 1)
                _p1_v2(v, NT - 1)
                v.wait_ge(lnx.sem, LNX_ALL)
                for ct in range(NCT):
                    v.reduce_max(amx[:, ct:ct + 1],
                                 krf[:, ct * 1024:(ct + 1) * 1024],
                                 axis=mybir.AxisListType.X, apply_absolute_value=True)
                qnt.inc(v.reduce_max(amx[:, 8:9], amx[:, 0:8],
                                     axis=mybir.AxisListType.X))
                v.wait_ge(qnt.sem, 2)
                vsf.inc(v.reduce_max(sc1[0:1, 32:33], ring[0:1, 0:128],
                                     axis=mybir.AxisListType.X))
                v.wait_ge(vsf.sem, vsf.n)
                vsf.inc(v.reciprocal(sc1[0:1, 33:34], sc1[0:1, 32:33]))
                v.wait_ge(vsf.sem, vsf.n)
                v.tensor_scalar(sc1[0:1, 34:35], sc1[0:1, 33:34],
                                ccol[0:1, 7:8], None, ALU.mult)
                qnt.inc(v.tensor_scalar(sc1[0:1, 35:36], sc1[0:1, 32:33],
                                        ccol[0:1, 13:14], None, ALU.mult))
                v.wait_ge(qnt.sem, 35)
                for ct in range(NCT):
                    kf = krf[:, ct * 1024:(ct + 1) * 1024]
                    v.tensor_scalar(junk[:], kf, scol[:, 0:1], ccol[:, 8:9],
                                    ALU.mult, ALU.add)
                    v.tensor_scalar(junk[:], junk[:], ccol[:, 9:10], ccol[:, 7:8],
                                    ALU.add, ALU.min)
                    v.tensor_scalar(qint[:, ct * 1024:(ct + 1) * 1024], junk[:],
                                    ccol[:, 10:11], None, ALU.max)
                qnt.inc(v.nop())
                v.wait_ge(pre.sem, PRE_ALL)
                qnt.inc(v.tensor_scalar(baugb[:], baug[:], scol[0:17, 0:1], None,
                                        ALU.mult))
                v.wait_ge(aux.sem, 34)
                qnt.inc(v.tensor_scalar(hcol[:], hcol[:], ccol[:, 0:1], None,
                                        ALU.add))
                v.tensor_scalar(bqs[:], bqs[:], ccol[:, 6:7], None, ALU.mult)
                v.tensor_scalar(imat_rs[:], imat_s[:], scol[:, 0:1], None, ALU.mult)
            def p0_v2(v):
                v.wait_ge(aux.sem, 1)
                aux.inc(v.tensor_copy(strip_s[:], ring[0:16, 0:512]))
            block.vector(p0_v2)
            block.vector(p1_v)

            def p1_s(s):
                _preload_named_set(nc, s, "natural_log_exp_and_others")
                for st in range(NT):
                    sl = st % 4
                    cb = sl * 8
                    varc = lncols[:, cb + 4:cb + 5]
                    lnv = lncols[:, cb + 7:cb + 8]
                    rstd = lncols[:, cb + 6:cb + 7]
                    s.wait_ge(v1.sem, st + 1)
                    ssf.inc(s.activation(out=lnv, in_=varc, func=AF.Ln,
                                         bias=ccol[:, 2:3], scale=ccol[:, 3:4]))
                    s.wait_ge(ssf.sem, ssf.n)
                    rst.inc(s.activation(out=rstd, in_=lnv, func=AF.Exp,
                                         bias=ccol[:, 12:13], scale=ccol[:, 4:5]))
            block.scalar(p1_s)

            def p0_t(t):
                t.wait_ge(pre.sem, 32)
                aux.inc(t.matmul(ring[0:16, 0:512], relb_s[:], oh_s[:],
                                 start=True, stop=True))
                t.wait_ge(qnt.sem, 1)
                t.wait_ge(aux.sem, 2)
                qnt.inc(t.matmul(ring[0:1, 0:128], amx[:, 8:9],
                                 imat_s[:], start=True, stop=True))
            block.tensor(p0_t)

            def pg_mid(g):
                g.wait_ge(aux.sem, 2)
                aux.inc(g.dma_start(out=strip_hb.ap(), in_=strip_s[:]), 16)
                g.wait_ge(aux.sem, 18)
                aux.inc(g.dma_start(
                    out=hcol[:], in_=AP(strip_hb, 64, [[0, 128], [512, H]]),
                    allow_slow_non_contiguous=True), 16)
                for st in range(NT):
                    sl = st % 4
                    g.wait_ge(v2.sem, st + 1)
                    xnw.inc(g.dma_start(
                        out=xn_hb.ap()[st * 128:(st + 1) * 128, :],
                        in_=xnr[:, sl * 1024:(sl + 1) * 1024]), 16)
                g.wait_ge(qnt.sem, 3)
                g.dma_start(out=AP(sc_hb, 0, [[1, 1], [1, 2]]),
                            in_=sc1[0:1, 34:36]).then_inc(qnt.sem, 16)
                qnt.n += 16
                g.wait_ge(qnt.sem, 19)
                qnt.inc(g.dma_start(out=scol[:, 0:2],
                                    in_=AP(sc_hb, 0, [[0, 128], [1, 2]])), 16)
                if lvl < 4:
                    return
                for ct in range(NCT):
                    stld.inc(g.dma_start(
                        out=wbuf[:, ct * 1024:(ct + 1) * 1024],
                        in_=wq_h.ap()[ct * 128:(ct + 1) * 128, :]), 16)
                for k in range(N_FFN):
                    st, fc = k // 2, k % 2
                    gsl = k % 4
                    g.wait_ge(pev.sem, N_UT + k + 1)
                    pst.inc(g.dma_start(
                        out=g_hb.ap()[st, :, fc * 512:(fc + 1) * 512],
                        in_=g_ring[:, gsl * 512:(gsl + 1) * 512]), 16)
                if lvl < 5:
                    return
                for k in range(N_QK):
                    dt, sch = k // NQC, k % NQC
                    esl = k % 3
                    g.wait_ge(pev.sem, N_UT + N_FFN + k + 1)
                    pst.inc(g.dma_start(
                        out=qT_hb.ap()[dt, :, sch * 512:(sch + 1) * 512],
                        in_=evst[:, esl * 512:(esl + 1) * 512]), 16)
                g.wait_ge(prd.sem, N_UT + N_FFN + N_QK)
                for ct in range(NCT):
                    stld.inc(g.dma_start(
                        out=wbuf[:, ct * 1024:(ct + 1) * 1024],
                        in_=wk_h.ap()[ct * 128:(ct + 1) * 128, :]), 16)
                for k in range(N_QK):
                    dt, sch = k // NQC, k % NQC
                    esl = (N_QK + k) % 3
                    g.wait_ge(pev.sem, N_UT + N_FFN + N_QK + k + 1)
                    pst.inc(g.dma_start(
                        out=kT_hb.ap()[dt, :, sch * 512:(sch + 1) * 512],
                        in_=evst[:, esl * 512:(esl + 1) * 512]), 16)
                g.wait_ge(prd.sem, N_UT + N_FFN + 2 * N_QK)
                for ct in range(NCT):
                    stld.inc(g.dma_start(
                        out=wbuf[:, ct * 1024:(ct + 1) * 1024],
                        in_=wv_h.ap()[ct * 128:(ct + 1) * 128, :]), 16)
                for k in range(N_V):
                    st, dc = k // 2, k % 2
                    esl = (2 * N_QK + k) % 3
                    g.wait_ge(pev.sem, N_UT + N_FFN + 2 * N_QK + k + 1)
                    pst.inc(g.dma_start(
                        out=v_hb.ap()[2 * dc, st, :, :],
                        in_=evst[:, esl * 512:esl * 512 + 256]), 16)
                    pst.inc(g.dma_start(
                        out=v_hb.ap()[2 * dc + 1, st, :, :],
                        in_=evst[:, esl * 512 + 256:esl * 512 + 512]), 16)
            block.gpsimd(pg_mid)
            STLD_W = 3 * NCT * 16

            def p2_sy(sy):
                sy.wait_ge(xnw.sem, NT * 16)
                for ct in range(NCT):
                    xnt.inc(sy.dma_start_transpose(
                        xnT[:, ct * S:(ct + 1) * S],
                        xn_hb.ap()[:, ct * 128:(ct + 1) * 128]), 16)
            block.sync(p2_sy)
            XNT_ALL = NCT * 16

            def p34_t(t):
                t.wait_ge(xnt.sem, XNT_ALL)
                t.wait_ge(qnt.sem, 37)
                ch = 0
                if lvl < 4:
                    return
                for sch in range(NQC):
                    bk = (ring[:, 0:512], ring[:, 512:1024], ring[:, 1024:1536])[ch % 3]
                    if ch >= 3:
                        t.wait_ge(pev.sem, ch - 2)
                    mm = None
                    for ct in range(NCT):
                        mm = t.matmul(bk[0:16, :], a_s[:, ct * 16:(ct + 1) * 16],
                                      xnT[:, ct * S + sch * 512:ct * S + sch * 512 + 512],
                                      start=(ct == 0), stop=(ct == NCT - 1))
                    prd.inc(mm)
                    ch += 1
                for k in range(N_FFN):
                    st, fc = k // 2, k % 2
                    bk = (ring[:, 0:512], ring[:, 512:1024], ring[:, 1024:1536])[ch % 3]
                    if ch >= 3:
                        t.wait_ge(pev.sem, ch - 2)
                    for ct in range(NCT):
                        t.matmul(bk[:], xnT[:, ct * S + st * 128:ct * S + st * 128 + 128],
                                 qint[:, ct * 1024 + fc * 512:ct * 1024 + fc * 512 + 512],
                                 start=(ct == 0), stop=False)
                    mm = t.matmul(bk[:], uTa[:, st * 128:(st + 1) * 128],
                                  baugb[:, fc * 512:(fc + 1) * 512],
                                  start=False, stop=True)
                    prd.inc(mm)
                    ch += 1
                if lvl < 5:
                    return
                for wsel in range(2):
                    t.wait_ge(stld.sem, (wsel + 1) * NCT * 16)
                    for k in range(N_QK):
                        dt, sch = k // NQC, k % NQC
                        bk = (ring[:, 0:512], ring[:, 512:1024], ring[:, 1024:1536])[ch % 3]
                        t.wait_ge(pev.sem, ch - 2)
                        mm = None
                        for ct in range(NCT):
                            mm = t.matmul(
                                bk[:],
                                wbuf[:, ct * 1024 + dt * 128:ct * 1024 + dt * 128 + 128],
                                xnT[:, ct * S + sch * 512:ct * S + sch * 512 + 512],
                                start=(ct == 0), stop=(ct == NCT - 1))
                        prd.inc(mm)
                        ch += 1
                t.wait_ge(stld.sem, 3 * NCT * 16)
                for k in range(N_V):
                    st, dc = k // 2, k % 2
                    bk = (ring[:, 0:512], ring[:, 512:1024], ring[:, 1024:1536])[ch % 3]
                    t.wait_ge(pev.sem, ch - 2)
                    for ct in range(NCT):
                        t.matmul(bk[:], xnT[:, ct * S + st * 128:ct * S + st * 128 + 128],
                                 wbuf[:, ct * 1024 + dc * 512:ct * 1024 + dc * 512 + 512],
                                 start=(ct == 0), stop=False)
                    mm = t.matmul(bk[:], onesr[0:1, 0:128],
                                  bvrow[0:1, dc * 512:(dc + 1) * 512],
                                  start=False, stop=True)
                    prd.inc(mm)
                    ch += 1
            block.tensor(p34_t)

            def p34_v(v):
                ch = 0
                if lvl < 4:
                    return
                for sch in range(NQC):
                    bk = (ring[:, 0:512], ring[:, 512:1024], ring[:, 1024:1536])[ch % 3]
                    v.wait_ge(prd.sem, ch + 1)
                    pev.inc(v.tensor_copy(uTa[0:16, sch * 512:(sch + 1) * 512],
                                          bk[0:16, :]))
                    ch += 1
                for k in range(N_FFN):
                    bk = (ring[:, 0:512], ring[:, 512:1024], ring[:, 1024:1536])[ch % 3]
                    gsl = k % 4
                    v.wait_ge(rst.sem, NT + k + 1)
                    if k >= 4:
                        v.wait_ge(pst.sem, pst_after_store("g", k - 4))
                    pev.inc(v.scalar_tensor_tensor(
                        g_ring[:, gsl * 512:(gsl + 1) * 512], th_s[:],
                        ccol[:, 3:4], bk[:], ALU.add, ALU.mult))
                    ch += 1
                if lvl < 5:
                    return
                for wsel in range(2):
                    bias = bqs if wsel == 0 else bks
                    kind = "q" if wsel == 0 else "k"
                    for k in range(N_QK):
                        dt = k // NQC
                        q3 = wsel * N_QK + k
                        bk = (ring[:, 0:512], ring[:, 512:1024], ring[:, 1024:1536])[ch % 3]
                        esl = q3 % 3
                        v.wait_ge(prd.sem, ch + 1)
                        if q3 >= 3:
                            p3 = q3 - 3
                            v.wait_ge(pst.sem, pst_after_store(
                                "q" if p3 < N_QK else "k",
                                p3 if p3 < N_QK else p3 - N_QK))
                        if wsel == 0:
                            pev.inc(v.tensor_scalar(
                                evst[:, esl * 512:(esl + 1) * 512], bk[:],
                                ccol[:, 6:7], bias[:, dt:dt + 1],
                                ALU.mult, ALU.add))
                        else:
                            pev.inc(v.tensor_single_scalar(
                                evst[:, esl * 512:(esl + 1) * 512], bk[:],
                                bias[:, dt:dt + 1], ALU.add))
                        ch += 1
                for k in range(N_V):
                    q3 = 2 * N_QK + k
                    bk = (ring[:, 0:512], ring[:, 512:1024], ring[:, 1024:1536])[ch % 3]
                    esl = q3 % 3
                    v.wait_ge(prd.sem, ch + 1)
                    p3 = q3 - 3
                    v.wait_ge(pst.sem, pst_after_store(
                        "k" if p3 < 2 * N_QK else "v",
                        p3 - N_QK if p3 < 2 * N_QK else p3 - 2 * N_QK))
                    pev.inc(v.tensor_copy(evst[:, esl * 512:(esl + 1) * 512], bk[:]))
                    ch += 1
            block.vector(p34_v)

            def p4_s(s):
                if lvl < 4:
                    return
                s.preload_act_table(AF.Tanh)
                for k in range(N_FFN):
                    bk = (ring[:, 0:512], ring[:, 512:1024], ring[:, 1024:1536])[(N_UT + k) % 3]
                    s.wait_ge(prd.sem, N_UT + k + 1)
                    if k >= 1:
                        s.wait_ge(pev.sem, N_UT + k)
                    rst.inc(s.activation(out=th_s[:], in_=bk[:], func=AF.Tanh,
                                         bias=ccol[:, 12:13], scale=scol[:, 1:2]))
            block.scalar(p4_s)

        if lvl >= 6:
            es2 = contextlib.ExitStack()
            with es2:
                kTs = es2.enter_context(nc.sbuf_tensor("kTs", [128, 2 * 2 * S], BF16))
                v_s = es2.enter_context(nc.sbuf_tensor("v_s", [128, 2 * NT * 256], BF16))
                qsl = es2.enter_context(nc.sbuf_tensor("qsl", [128, 2 * 2 * 512], BF16))
                attnU = es2.enter_context(nc.sbuf_tensor("attnU", [128, 8 * 1024], BF16))
                btile = es2.enter_context(nc.sbuf_tensor("btile", [128, 2 * 8 * 128], BF16))
                sTst = es2.enter_context(nc.sbuf_tensor("sTst", [128, 4 * 512], BF16))
                dnst = es2.enter_context(nc.sbuf_tensor("dnst", [128, 2 * 512], F32))
                e_s = es2.enter_context(nc.sbuf_tensor("e_s", [128, 2 * 1024], BF16))
                e_g = es2.enter_context(nc.sbuf_tensor("e_g", [128, 2 * 1024], BF16))
                e_x = es2.enter_context(nc.sbuf_tensor("e_x", [128, 2 * 1024], F32))
                e_o = es2.enter_context(nc.sbuf_tensor("e_o", [128, 2 * 1024], F32))
                e_r = es2.enter_context(nc.sbuf_tensor("e_r", [128, 2 * 16], F32))
                denT = es2.enter_context(nc.sbuf_tensor("denT", [128, 512], F32))
                NW = 4
                W_PER_QC = 16 * NW

                def w_index(hq, qc, kp, hi):
                    return ((hq * NQC + qc) * 16 + kp) * NW + hi

                STLD_HQ = (2 + NT) * 16

                unit_info = []
                win_units = {}
                den_unit = {}
                dtx_unit = {}
                for hq_ in range(4):
                    for qc_ in range(NQC):
                        gqc_ = hq_ * NQC + qc_
                        for kp_ in range(16):
                            for hi_ in range(4):
                                w_ = w_index(hq_, qc_, kp_, hi_)
                                u0 = len(unit_info)
                                unit_info.append(("sc", w_))
                                unit_info.append(("sc", w_))
                                win_units[w_] = (u0, u0 + 1)
                            if kp_ >= 1:
                                den_unit[(gqc_, kp_ - 1)] = len(unit_info)
                                unit_info.append(("den", gqc_ * 16 + kp_ - 1))
                        den_unit[(gqc_, 15)] = len(unit_info)
                        unit_info.append(("den", gqc_ * 16 + 15))
                        dtx_unit[gqc_] = len(unit_info)
                        unit_info.append(("dtx", gqc_))

                def unit_free_wait(eng, u):
                    if u < 6:
                        return
                    kind, p = unit_info[u - 6]
                    if kind == "sc":
                        eng.wait_ge(ex.sem, p + 1)
                    elif kind == "den":
                        gq, kp_ = p // 16, p % 16
                        eng.wait_ge(dvd.sem, gq * 17 + kp_ + 1)
                    else:
                        eng.wait_ge(dvd.sem, (p + 1) * 17)

                def ubank(u):
                    return (u % 6) * 512

                def p6_sy(sy):
                    sy.wait_ge(pst.sem, PST_ALL)
                    for hq in range(4):
                        sset = hq % 2
                        if hq >= 2:
                            sy.wait_ge(cns.sem, (hq - 1) * NQC * 16)
                        for dt2 in range(2):
                            stld.inc(sy.dma_start(
                                out=kTs[:, (sset * 2 + dt2) * S:(sset * 2 + dt2 + 1) * S],
                                in_=kT_hb.ap()[2 * hq + dt2, :, :]), 16)
                        for kt in range(NT):
                            stld.inc(sy.dma_start(
                                out=v_s[:, sset * NT * 256 + kt * 256:
                                        sset * NT * 256 + (kt + 1) * 256],
                                in_=v_hb.ap()[hq, kt, :, :]), 16)
                        for qc in range(NQC):
                            gqc = hq * NQC + qc
                            qset = gqc % 2
                            if gqc >= 2:
                                sy.wait_ge(sc.sem, (gqc - 1) * W_PER_QC)
                            for dt2 in range(2):
                                qld.inc(sy.dma_start(
                                    out=qsl[:, (qset * 2 + dt2) * 512:
                                            (qset * 2 + dt2 + 1) * 512],
                                    in_=qT_hb.ap()[2 * hq + dt2, :,
                                                   qc * 512:(qc + 1) * 512]), 16)
                block.sync(p6_sy)

                def p6_g(g):
                    g.wait_ge(aux.sem, 18)
                    for hq in range(4):
                        bset = hq % 2
                        if hq >= 2:
                            g.wait_ge(cns.sem, (hq - 1) * NQC * 16)
                        for hi in range(4):
                            h = 4 * hq + hi
                            for dt in range(2):
                                j0 = 128 * dt + 1
                                btl.inc(g.dma_start(
                                    out=btile[:, (bset * 8 + hi * 2 + dt) * 128:
                                              (bset * 8 + hi * 2 + dt + 1) * 128],
                                    in_=AP(strip_hb, h * 512 + j0, [[1, 128], [1, 128]])), 16)
                        for qc in range(NQC):
                            gqc = hq * NQC + qc
                            esl = gqc % 2
                            hp = 2 * hq
                            g.wait_ge(epi.sem, gqc * 2 + 1)
                            est.inc(g.dma_start(
                                out=sT_hb.ap()[hp * 128:(hp + 1) * 128,
                                               qc * 512:(qc + 1) * 512],
                                in_=sTst[:, esl * 1024:esl * 1024 + 512]), 16)
                            g.wait_ge(epi.sem, gqc * 2 + 2)
                            est.inc(g.dma_start(
                                out=sT_hb.ap()[(hp + 1) * 128:(hp + 2) * 128,
                                               qc * 512:(qc + 1) * 512],
                                in_=sTst[:, esl * 1024 + 512:esl * 1024 + 1024]), 16)
                block.gpsimd(p6_g)

                def _p6_cons(t, hq, qc, kp, sset):
                    gqc = hq * NQC + qc
                    kpg = gqc * 16 + kp
                    t.wait_ge(ex.sem, kpg * NW + NW)
                    if kp == 0 and gqc >= 1:
                        t.wait_ge(epi.sem, gqc * 2)
                    first = (kp == 0)
                    lastg = (kp == 15)
                    lastctx = None
                    for j in range(2):
                        kt = 2 * kp + j
                        for pi, ctx in ((0, ctxA), (1, ctxB)):
                            for ci in range(2):
                                hi = pi * 2 + ci
                                w = w_index(hq, qc, kp, hi)
                                au = attnU[:, (w % 8) * 1024 + j * 512:
                                           (w % 8) * 1024 + j * 512 + 512]
                                lastctx = t.matmul(
                                         ctx[ci * 64:(ci + 1) * 64, :],
                                         v_s[:, sset * NT * 256 + kt * 256 + hi * 64:
                                             sset * NT * 256 + kt * 256 + hi * 64 + 64],
                                         au,
                                         start=(first and j == 0),
                                         stop=(lastg and j == 1),
                                         tile_position=(0, ci * 64))
                    cns.inc(lastctx)
                    du = den_unit[(gqc, kp)]
                    unit_free_wait(t, du)
                    boff = ubank(du)
                    last = None
                    for j in range(2):
                        for hi in range(4):
                            w = w_index(hq, qc, kp, hi)
                            au = attnU[:, (w % 8) * 1024 + j * 512:
                                       (w % 8) * 1024 + j * 512 + 512]
                            last = t.matmul(ring[32 * hi:32 * (hi + 1),
                                                 boff:boff + 512], twos_s[:], au,
                                            start=(j == 0), stop=(j == 1),
                                            tile_position=(0, 32 * hi))
                    dnu.inc(last)

                def p6_t(t):
                    t.wait_ge(pev.sem, N_CH)
                    for hq in range(4):
                        sset = hq % 2
                        bset = hq % 2
                        t.wait_ge(stld.sem, STLD_W + (hq + 1) * STLD_HQ)
                        t.wait_ge(btl.sem, (hq + 1) * 8 * 16)
                        for qc in range(NQC):
                            gqc = hq * NQC + qc
                            qset = gqc % 2
                            t.wait_ge(qld.sem, (gqc + 1) * 2 * 16)
                            for kp in range(16):
                                def _score_mm(t, hi, j, stop):
                                    kt = 2 * kp + j
                                    w = w_index(hq, qc, kp, hi)
                                    boff = ubank(win_units[w][j])
                                    dtl = sset * 2 + hi // 2
                                    rb = (hi % 2) * 64
                                    return t.matmul(
                                        ring[:, boff:boff + 512],
                                        kTs[rb:rb + 64, dtl * S + kt * 128:
                                            dtl * S + kt * 128 + 128],
                                        qsl[rb:rb + 64,
                                            (qset * 2 + hi // 2) * 512:
                                            (qset * 2 + hi // 2 + 1) * 512],
                                        start=True, stop=stop,
                                        tile_position=((hi % 2) * 64, 0))

                                any_diag = any(_kt_class(2 * kp + j, qc) == "diag"
                                               for j in range(2))
                                if not any_diag:
                                    for ha in (0, 2):
                                        hb = ha + 1
                                        wA = w_index(hq, qc, kp, ha)
                                        wB = w_index(hq, qc, kp, hb)
                                        for j in range(2):
                                            unit_free_wait(t, win_units[wA][j])
                                            unit_free_wait(t, win_units[wB][j])
                                            mmA = _score_mm(t, ha, j, True)
                                            mmB = _score_mm(t, hb, j, True)
                                            if j == 1:
                                                sc.inc(mmA)
                                                sc.inc(mmB)
                                    if kp >= 1:
                                        _p6_cons(t, hq, qc, kp - 1, sset)
                                    continue
                                for hi in range(4):
                                    w = w_index(hq, qc, kp, hi)
                                    uj = win_units[w]
                                    last = None
                                    for j in range(2):
                                        kt = 2 * kp + j
                                        unit_free_wait(t, uj[j])
                                        boff = ubank(uj[j])
                                        cls = _kt_class(kt, qc)
                                        last = _score_mm(t, hi, j,
                                                         cls != "diag")
                                        if cls == "diag":
                                            subs = [qt - 4 * qc for qt in (kt, kt + 1)
                                                    if 4 * qc <= qt <= 4 * qc + 3]
                                            for idx, sub in enumerate(subs):
                                                dt = (sub + 4 * qc) - kt
                                                last = t.matmul(
                                                    ring[:, boff + sub * 128:
                                                         boff + sub * 128 + 128],
                                                    jmat_s[:],
                                                    btile[:, (bset * 8 + hi * 2 + dt) * 128:
                                                          (bset * 8 + hi * 2 + dt + 1) * 128],
                                                    start=False,
                                                    stop=(idx == len(subs) - 1))
                                    sc.inc(last)
                                if kp >= 1:
                                    _p6_cons(t, hq, qc, kp - 1, sset)
                            _p6_cons(t, hq, qc, 15, sset)
                            t.wait_ge(dvd.sem, gqc * 17 + 16)
                            tu = dtx_unit[gqc]
                            unit_free_wait(t, tu)
                            tb = ubank(tu)
                            mm = None
                            for i in range(4):
                                mm = t.matmul(
                                    ring[:, tb + i * 128:tb + (i + 1) * 128],
                                    dnst[:, (gqc % 2) * 512 + i * 128:
                                         (gqc % 2) * 512 + (i + 1) * 128],
                                    imat_rs[:], start=True, stop=True)
                            dnu.inc(mm)
                block.tensor(p6_t)

                def p6_s(s):
                    _preload_named_set(nc, s, "exp_and_others")
                    for hq in range(4):
                        for qc in range(NQC):
                            for kp in range(16):
                                kpg = (hq * NQC + qc) * 16 + kp
                                if kpg >= 2:
                                    pk = kpg - 2
                                    s.wait_ge(dnu.sem,
                                              (pk // 16) * 17 + (pk % 16) + 1)
                                for hi in range(4):
                                    h = 4 * hq + hi
                                    w = w_index(hq, qc, kp, hi)
                                    uj = win_units[w]
                                    b0, b1 = ubank(uj[0]), ubank(uj[1])
                                    s.wait_ge(sc.sem, w + 1)
                                    cls = [_kt_class(2 * kp + j, qc) for j in range(2)]
                                    def bias_of(c):
                                        return (hcol[:, h:h + 1] if c == "fut"
                                                else ccol[:, 0:1])
                                    dsl = (w % 8) * 1024
                                    same_bias = (cls[0] == "fut") == (cls[1] == "fut")
                                    contig = (b1 == b0 + 512)
                                    if same_bias and contig:
                                        last = s.activation(
                                            out=attnU[:, dsl:dsl + 1024],
                                            in_=ring[:, b0:b0 + 1024],
                                            func=AF.Exp, bias=bias_of(cls[0]))
                                    else:
                                        s.activation(
                                            out=attnU[:, dsl:dsl + 512],
                                            in_=ring[:, b0:b0 + 512],
                                            func=AF.Exp, bias=bias_of(cls[0]))
                                        last = s.activation(
                                            out=attnU[:, dsl + 512:dsl + 1024],
                                            in_=ring[:, b1:b1 + 512],
                                            func=AF.Exp, bias=bias_of(cls[1]))
                                    ex.inc(last)
                block.scalar(p6_s)

                def p6_v(v):
                    for hq in range(4):
                        for qc in range(NQC):
                            gqc = hq * NQC + qc
                            esl = gqc % 2
                            for kp in range(16):
                                du = den_unit[(gqc, kp)]
                                boff = ubank(du)
                                v.wait_ge(dnu.sem, gqc * 17 + kp + 1)
                                dst = dnst[:, esl * 512:(esl + 1) * 512]
                                if kp == 0:
                                    dvd.inc(v.tensor_copy(dst, ring[:, boff:boff + 512]))
                                else:
                                    dvd.inc(v.tensor_tensor(dst, dst,
                                                            ring[:, boff:boff + 512],
                                                            ALU.add))
                            v.wait_ge(cns.sem, (gqc + 1) * 16)
                            if gqc >= 2:
                                v.wait_ge(est.sem, (gqc - 1) * 2 * 16)
                            epi.inc(v.tensor_copy(sTst[:, esl * 1024:esl * 1024 + 512],
                                                  ctxA[:]))
                            epi.inc(v.tensor_copy(sTst[:, esl * 1024 + 512:
                                                        esl * 1024 + 1024], ctxB[:]))
                            v.wait_ge(dnu.sem, (gqc + 1) * 17)
                            tb = ubank(dtx_unit[gqc])
                            dvd.inc(v.tensor_copy(
                                AP(denT, qc * 64 + 4 * hq,
                                   [[512, 128], [16, 4], [1, 4]]),
                                AP(ring, tb, [[3072, 128], [128, 4], [32, 4]])))
                    for qt in range(NT):
                        esl = qt % 2
                        v.wait_ge(eld.sem, (qt + 1) * 3 * 16)
                        if qt >= 2:
                            v.wait_ge(ost.sem, (qt - 1) * 16)
                        vsf.inc(v.reciprocal(e_r[:, esl * 16:(esl + 1) * 16],
                                             denT[:, qt * 16:(qt + 1) * 16]))
                        v.wait_ge(vsf.sem, vsf.n)
                        rE = AP(e_r, esl * 16, [[32, 128], [1, 16], [0, 64]])
                        if dump == "edn":
                            ecp.inc(v.tensor_copy(
                                e_o[:, esl * 1024:esl * 1024 + 512], denT[:]))
                            continue
                        if dump == "es":
                            ecp.inc(v.tensor_copy(
                                e_o[:, esl * 1024:(esl + 1) * 1024],
                                e_s[:, esl * 1024:(esl + 1) * 1024]))
                            continue
                        if dump == "eg":
                            ecp.inc(v.tensor_copy(
                                e_o[:, esl * 1024:(esl + 1) * 1024],
                                e_g[:, esl * 1024:(esl + 1) * 1024]))
                            continue
                        if dump == "er":
                            ecp.inc(v.tensor_copy(
                                e_o[:, esl * 1024:(esl + 1) * 1024], rE))
                            continue
                        v.tensor_tensor(e_o[:, esl * 1024:(esl + 1) * 1024],
                                        e_s[:, esl * 1024:(esl + 1) * 1024], rE,
                                        ALU.mult)
                        v.tensor_tensor(e_o[:, esl * 1024:(esl + 1) * 1024],
                                        e_o[:, esl * 1024:(esl + 1) * 1024],
                                        e_g[:, esl * 1024:(esl + 1) * 1024], ALU.mult)
                        ecp.inc(v.tensor_tensor(e_o[:, esl * 1024:(esl + 1) * 1024],
                                                e_o[:, esl * 1024:(esl + 1) * 1024],
                                                e_x[:, esl * 1024:(esl + 1) * 1024],
                                                ALU.add))
                block.vector(p6_v)

                def p7_sy(sy):
                    for qt in range(NT):
                        esl = qt % 2
                        qcq = qt // 4
                        sy.wait_ge(est.sem, (3 * NQC + qcq + 1) * 2 * 16)
                        if qt >= 2:
                            sy.wait_ge(ecp.sem, qt - 1)
                        eld.inc(sy.dma_start_transpose(
                            e_s[:, esl * 1024:(esl + 1) * 1024],
                            sT_hb.ap()[:, qt * 128:(qt + 1) * 128]), 16)
                        eld.inc(sy.dma_start(out=e_g[:, esl * 1024:(esl + 1) * 1024],
                                             in_=g_hb.ap()[qt, :, :]), 16)
                        eld.inc(sy.dma_start(out=e_x[:, esl * 1024:(esl + 1) * 1024],
                                             in_=x_h.ap()[qt * 128:(qt + 1) * 128, :]), 16)
                    sy.wait_ge(ost.sem, NT * 16)
                block.sync(p7_sy)

                def p7_g(g):
                    for qt in range(NT):
                        esl = qt % 2
                        g.wait_ge(ecp.sem, qt + 1)
                        ost.inc(g.dma_start(out=out_h.ap()[qt * 128:(qt + 1) * 128, :],
                                            in_=e_o[:, esl * 1024:(esl + 1) * 1024]), 16)
                block.gpsimd(p7_g)

        if lvl < 7:
            _sdmp = es.enter_context(nc.semaphore("s_dump"))
            dmp = Ctr(_sdmp)

            def pdump_g(g):
                g.wait_ge(qnt.sem, 38)
                dmp.inc(g.dma_start(out=dbg_h.ap()[:, 0:32], in_=lncols[:]), 16)
                dmp.inc(g.dma_start(out=dbg_h.ap()[:, 32:36], in_=scol[:]), 16)
                dmp.inc(g.dma_start(out=dbg_h.ap()[0:1, 36:52],
                                    in_=sc1[0:1, 28:44]), 16)
                if dump == "xn":
                    g.wait_ge(xnw.sem, NT * 16)
                    g.wait_ge(xnt.sem, XNT_ALL)
                    for st in range(NT):
                        dmp.inc(g.dma_start(
                            out=out_h.ap()[st * 128:(st + 1) * 128, :],
                            in_=xn_hb.ap()[st * 128:(st + 1) * 128, :]), 16)
                elif dump == "qint":
                    g.wait_ge(qnt.sem, 36)
                    for ct in range(NCT):
                        dmp.inc(g.dma_start(
                            out=out_h.ap()[ct * 128:(ct + 1) * 128, :],
                            in_=qint[:, ct * 1024:(ct + 1) * 1024]), 16)
                elif dump == "g":
                    g.wait_ge(pst.sem, N_FFN * 16)
                    dmp.inc(g.dma_start(
                        out=AP(out_h, 0, [[Dm, NT * 128], [1, Dm]]),
                        in_=g_hb.ap().rearrange("a p b -> (a p) b")), 16)
                elif dump == "qT":
                    g.wait_ge(pst.sem, (N_FFN + N_QK) * 16)
                    dmp.inc(g.dma_start(
                        out=AP(out_h, 0, [[S, NCT * 128], [1, S]]),
                        in_=qT_hb.ap().rearrange("a p b -> (a p) b")), 16)
                elif dump == "kT":
                    g.wait_ge(pst.sem, (N_FFN + 2 * N_QK) * 16)
                    dmp.inc(g.dma_start(
                        out=AP(out_h, 0, [[S, NCT * 128], [1, S]]),
                        in_=kT_hb.ap().rearrange("a p b -> (a p) b")), 16)
                elif dump == "v":
                    g.wait_ge(pst.sem, PST_ALL)
                    dmp.inc(g.dma_start(
                        out=AP(out_h, 0, [[256, 4 * NT * 128], [1, 256]]),
                        in_=v_hb.ap().rearrange("a b p c -> (a b p) c")), 16)
                elif dump == "sT":
                    g.wait_ge(est.sem, est.n)
                    dmp.inc(g.dma_start(
                        out=AP(out_h, 0, [[S, Dm], [1, S]]),
                        in_=sT_hb.ap()), 16)
                elif dump == "den":
                    g.wait_ge(est.sem, est.n)
                    dmp.inc(g.dma_start(
                        out=AP(out_h, 0, [[S, H], [1, S]]),
                        in_=den_hb.ap()), 16)
            block.gpsimd(pdump_g)

            def pdump_sy(sy):
                if dmp.n:
                    sy.wait_ge(dmp.sem, dmp.n)
                if dump == "w0":
                    sy.wait_ge(dbgw.sem, 33)
            block.sync(pdump_sy)

    return nc


_NC_CACHE = None


def kernel(**inputs):
    global _NC_CACHE
    oh, jmat, imat = _host_consts()
    x = np.ascontiguousarray(inputs["L0_attn_hidden_states"], dtype=np.float32)
    base = {
        "wq": np.ascontiguousarray(inputs["L0_attn_Wq"], dtype=np.float32),
        "bq": np.ascontiguousarray(inputs["L0_attn_bq"], dtype=np.float32),
        "wk": np.ascontiguousarray(inputs["L0_attn_Wk"], dtype=np.float32),
        "bk": np.ascontiguousarray(inputs["L0_attn_bk"], dtype=np.float32),
        "wv": np.ascontiguousarray(inputs["L0_attn_Wv"], dtype=np.float32),
        "bv": np.ascontiguousarray(inputs["L0_attn_bv"], dtype=np.float32),
        "relb": np.ascontiguousarray(inputs["L0_attn_rel_bias"], dtype=np.float32),
        "ker": np.ascontiguousarray(inputs["L0_ffn_kernel"], dtype=np.float32),
        "fbias": np.ascontiguousarray(inputs["L0_ffn_bias"], dtype=np.float32),
        "a_w": np.ascontiguousarray(inputs["L0_ffn_a"], dtype=np.float32),
        "b_w": np.ascontiguousarray(inputs["L0_ffn_b"], dtype=np.float32),
        "onehot": oh, "jmat": jmat, "imat": imat,
    }
    import os
    lvl = int(os.environ.get("KLVL", "7"))
    dump = os.environ.get("KDUMP") or None
    if _NC_CACHE is None:
        _NC_CACHE = build_nc(lvl, dump)
    nc = _NC_CACHE
    in_maps = [dict(base, x=np.ascontiguousarray(x[b])) for b in range(8)]
    res = run_bass_kernel_spmd(nc, in_maps, core_ids=list(range(8)))
    return np.stack([res.results[b]["out"] for b in range(8)], axis=0)
